# version 18
# baseline (speedup 1.0000x reference)
import math
from contextlib import ExitStack

import numpy as np
import concourse.bass as bass
import concourse.mybir as mybir
from concourse.bass_utils import run_bass_kernel_spmd

F32 = mybir.dt.float32
BF16 = mybir.dt.bfloat16
AF = mybir.ActivationFunctionType
ALU = mybir.AluOpType
AXX = mybir.AxisListType.X

S = 8192
DM = 2048
HD = 128
NCORE = 8
HPC = 2
TPC = S // NCORE
NDC = DM // 128
NT = S // 128
NTB = S // 512
FE = 1408
FS = 2816
NEXP = 8
EXPB = -8.0
SCL = 1.0 / math.sqrt(HD)
BIG = 1.0e30
EPS = 1e-5


class P:

    def __init__(self, nc, stack, names):
        self.nc = nc
        self.h = {}
        self.c = {}
        for n in names:
            self.h[n] = stack.enter_context(nc.semaphore(n))
            self.c[n] = 0

    def inc(self, instr, name, k=1):
        instr.then_inc(self.h[name], k)
        self.c[name] += k
        return self.c[name]

    def dma(self, eng, out, in_, name="ld"):
        return self.inc(eng.dma_start(out=out, in_=in_), name, 16)

    def wait(self, eng, name, val=None):
        eng.wait_ge(self.h[name], self.c[name] if val is None else val)

    vdrain = True

    def v(self, f):
        if self.vdrain:
            self.wait(self.nc.vector, "vc")
        return self.inc(f(), "vc")

    def s(self, f):
        return self.inc(f(), "sa")


def barrier(nc, p):
    engines = (nc.tensor, nc.vector, nc.scalar, nc.gpsimd, nc.sync)
    for nm in p.h:
        if nm in ("tm", "vc", "sa", "ph"):
            continue
        p.wait(nc.gpsimd, nm)
    for e in engines:
        p.inc(e.nop(), "ph", 1)
    for e in engines:
        p.wait(e, "ph")


def split_hi_lo(nc, p, src_ap, hi_ap, lo_ap):
    p.wait(nc.scalar, "vc")
    va = p.s(lambda: nc.scalar.activation(out=hi_ap, in_=src_ap, func=AF.Copy))
    p.wait(nc.vector, "sa", va)
    p.v(lambda: nc.vector.tensor_tensor(out=lo_ap, in0=src_ap, in1=hi_ap,
                                  op=ALU.subtract))


def split_w(nc, p, src_ap, hi_ap, lo_ap):
    va = p.s(lambda: nc.scalar.activation(out=hi_ap, in_=src_ap, func=AF.Copy))
    p.wait(nc.vector, "sa", va)
    p.v(lambda: nc.vector.tensor_tensor(out=lo_ap, in0=src_ap, in1=hi_ap,
                                  op=ALU.subtract))


def mm3(nc, p, out_ap, lh, ll, rh, rl, start, stop):
    p.inc(nc.tensor.matmul(out_ap, lh, rh, start=start, stop=False), "tm")
    p.inc(nc.tensor.matmul(out_ap, lh, rl, start=False, stop=False), "tm")
    p.inc(nc.tensor.matmul(out_ap, ll, rh, start=False, stop=stop), "tm")


def ln_rows(nc, p, ps_s, ps_q, trow, trow2, rrow, rstd_out, mr_out, eps_ap):
    p.v(lambda: nc.vector.tensor_scalar_mul(trow, ps_s, 1.0 / DM))
    p.v(lambda: nc.vector.tensor_scalar_mul(trow2, ps_q, 1.0 / DM))
    p.v(lambda: nc.vector.tensor_tensor(out=rrow, in0=trow, in1=trow, op=ALU.mult))
    p.v(lambda: nc.vector.tensor_tensor(out=trow2, in0=trow2, in1=rrow,
                                  op=ALU.subtract))
    va = p.c["vc"]
    p.wait(nc.scalar, "vc", va)
    sa = p.s(lambda: nc.scalar.activation(out=rrow, in_=trow2, func=AF.Sqrt,
                                    bias=eps_ap))
    p.wait(nc.vector, "sa", sa)
    p.v(lambda: nc.vector.reciprocal(out=rstd_out, in_=rrow))
    p.v(lambda: nc.vector.tensor_tensor(out=mr_out, in0=trow, in1=rstd_out,
                                  op=ALU.mult))


def build_full(debug=False):
    nc = bass.Bass("TRN2", target_bir_lowering=False, dynamic_dma_scratch_size=2048)
    dbg = dict(kind="ExternalOutput") if debug else {}
    xT = nc.dram_tensor("xT", [DM, S], F32, kind="ExternalInput").ap()
    wq = nc.dram_tensor("wq", [DM, HPC * HD], F32, kind="ExternalInput").ap()
    wk = nc.dram_tensor("wk", [DM, HPC * HD], F32, kind="ExternalInput").ap()
    wv = nc.dram_tensor("wv", [DM, HPC * HD], F32, kind="ExternalInput").ap()
    wo = nc.dram_tensor("wo", [HPC * HD, DM], F32, kind="ExternalInput").ap()
    mask01 = nc.dram_tensor("mask01", [128, 128], F32, kind="ExternalInput").ap()
    xTm = nc.dram_tensor("xTm", [DM, TPC], F32, kind="ExternalInput").ap()
    wgT = nc.dram_tensor("wgT", [NEXP, DM, FE], F32, kind="ExternalInput").ap()
    wuT = nc.dram_tensor("wuT", [NEXP, DM, FE], F32, kind="ExternalInput").ap()
    wdT = nc.dram_tensor("wdT", [NEXP, FE, DM], F32, kind="ExternalInput").ap()
    wsgT = nc.dram_tensor("wsgT", [DM, FS], F32, kind="ExternalInput").ap()
    wsuT = nc.dram_tensor("wsuT", [DM, FS], F32, kind="ExternalInput").ap()
    wsdT = nc.dram_tensor("wsdT", [FS, DM], F32, kind="ExternalInput").ap()
    wrT = nc.dram_tensor("wrT", [DM, 8], F32, kind="ExternalInput").ap()
    identf = nc.dram_tensor("identf", [128, 128], F32, kind="ExternalInput").ap()
    y = nc.dram_tensor("y", [TPC, DM], F32, kind="ExternalOutput").ap()
    rs_g = [nc.dram_tensor(f"rs_g{g}", [NCORE, 512, TPC], F32).ap()
            for g in range(4)]
    rso_g = [nc.dram_tensor(f"rso_g{g}", [512, TPC], F32).ap() for g in range(4)]

    ag_in = nc.dram_tensor("ag_in", [2, TPC], F32).ap()
    stats_g = nc.dram_tensor("stats_g", [NCORE * 2, TPC], F32).ap()
    d_xnew = nc.dram_tensor("d_xnew", [128, NDC * TPC], F32, **dbg).ap()
    d_rm = nc.dram_tensor("d_rm", [2, TPC], F32, **dbg).ap()
    d_lnh = nc.dram_tensor("d_lnh", [128, NDC * 512], BF16, **dbg).ap()
    d_lnl = nc.dram_tensor("d_lnl", [128, NDC * 512], BF16, **dbg).ap()
    d_hh = nc.dram_tensor("d_hh", [128, 11 * 512], BF16, **dbg).ap()
    d_hl = nc.dram_tensor("d_hl", [128, 11 * 512], BF16, **dbg).ap()
    d_dwh = nc.dram_tensor("d_dwh", [128, FE], BF16, **dbg).ap()
    d_dwl = nc.dram_tensor("d_dwl", [128, FE], BF16, **dbg).ap()
    d_x2 = nc.dram_tensor("d_x2", [128, NDC * TPC], F32, **dbg).ap()
    d_x3 = nc.dram_tensor("d_x3", [128, NDC * TPC], F32, **dbg).ap()
    qhi_d = nc.dram_tensor("qhi_d", [HPC, HD, S], BF16, **dbg).ap()
    qlo_d = nc.dram_tensor("qlo_d", [HPC, HD, S], BF16, **dbg).ap()
    khi_d = nc.dram_tensor("khi_d", [HPC, HD, S], BF16, **dbg).ap()
    klo_d = nc.dram_tensor("klo_d", [HPC, HD, S], BF16, **dbg).ap()
    vhi_d = nc.dram_tensor("vhi_d", [HPC, 128, S], BF16, **dbg).ap()
    vlo_d = nc.dram_tensor("vlo_d", [HPC, 128, S], BF16, **dbg).ap()

    with ExitStack() as st:
        E = st.enter_context
        p = P(nc, st, ["ld", "lq", "lq0", "lq1", "lkv", "lx0", "lx1", "lw",
                       "lp0", "lp1", "lp2", "lp3", "sd", "so0", "so1", "lg",
                       "tm", "vc", "sa", "ph", "cc"])

        p.vdrain = False
        ones_r = E(nc.sbuf_tensor("ones_r", [1, 128], F32))
        ones_c = E(nc.sbuf_tensor("ones_c", [128, 1], BF16))
        mask_s = E(nc.sbuf_tensor("mask_s", [128, 128], F32))
        eps1 = E(nc.sbuf_tensor("eps1", [1, 1], F32))
        sclc = E(nc.sbuf_tensor("sclc", [128, 1], F32))
        expbc = E(nc.sbuf_tensor("expbc", [128, 1], F32))
        ident_s = E(nc.sbuf_tensor("ident_s", [128, 128], F32))
        ascope = ExitStack()
        o_hi = [ascope.enter_context(nc.sbuf_tensor(f"o_hi{h}", [128, S], BF16))
                for h in range(HPC)]
        o_lo = [ascope.enter_context(nc.sbuf_tensor(f"o_lo{h}", [128, S], BF16))
                for h in range(HPC)]

        p.v(lambda: nc.vector.memset(ones_r[:], 1.0))
        p.v(lambda: nc.vector.memset(ones_c[:], 1.0))
        p.v(lambda: nc.vector.memset(eps1[:], EPS))
        p.v(lambda: nc.vector.memset(sclc[:], SCL))
        p.v(lambda: nc.vector.memset(expbc[:], EXPB))
        init_vc = p.c["vc"]
        p.wait(nc.scalar, "vc", init_vc)
        p.dma(nc.sync, mask_s[:], mask01)
        p.dma(nc.sync, ident_s[:], identf)

        with ExitStack() as ph:
            Eh = ph.enter_context
            xts = [Eh(nc.sbuf_tensor(f"a0x{i}", [128, 512], F32)) for i in range(2)]
            xbf = [Eh(nc.sbuf_tensor(f"a0b{i}", [128, 512], BF16)) for i in range(2)]
            sqb = [Eh(nc.sbuf_tensor(f"a0q{i}", [128, 512], BF16)) for i in range(2)]
            trow = Eh(nc.sbuf_tensor("a0t1", [1, 512], F32))
            trow2 = Eh(nc.sbuf_tensor("a0t2", [1, 512], F32))
            rrow = Eh(nc.sbuf_tensor("a0r", [1, 512], F32))
            mrow = Eh(nc.sbuf_tensor("a0m", [1, 512], F32))
            ps_s = Eh(nc.psum_tensor("a0ps", [1, 512], F32))
            ps_q = Eh(nc.psum_tensor("a0pq", [1, 512], F32))

            p.wait(nc.tensor, "vc", init_vc)
            it_tm = {}
            row_sd = 0
            for tb in range(2):
                for dc in range(NDC):
                    it = tb * NDC + dc
                    i = it % 2
                    if it >= 2:
                        p.wait(nc.sync, "tm", it_tm[it - 2])
                    v = p.dma(nc.sync, xts[i][:],
                              xTm[dc * 128:(dc + 1) * 128, tb * 512:(tb + 1) * 512],
                              f"lx{i}")
                    p.wait(nc.scalar, f"lx{i}", v)
                    if it >= 2:
                        p.wait(nc.scalar, "tm", it_tm[it - 2])
                    p.s(lambda: nc.scalar.activation(out=xbf[i][:], in_=xts[i][:],
                                               func=AF.Copy))
                    sa = p.s(lambda: nc.scalar.activation(out=sqb[i][:], in_=xts[i][:],
                                                    func=AF.Square))
                    p.wait(nc.tensor, "sa", sa)
                    p.inc(nc.tensor.matmul(ps_s[:], ones_c[:], xbf[i][:],
                                           start=(dc == 0), stop=(dc == NDC - 1)), "tm")
                    it_tm[it] = p.inc(nc.tensor.matmul(ps_q[:], ones_c[:], sqb[i][:],
                                                       start=(dc == 0),
                                                       stop=(dc == NDC - 1)), "tm")
                p.wait(nc.vector, "tm", p.c["tm"])
                if tb >= 1:
                    p.wait(nc.vector, "sd", row_sd)
                    p.wait(nc.scalar, "sd", row_sd)
                ln_rows(nc, p, ps_s[:], ps_q[:], trow[:], trow2[:], rrow[:],
                        mrow[:], trow[:], eps1[:])
                p.wait(nc.tensor, "vc", p.c["vc"])
                p.wait(nc.scalar, "vc")
                p.dma(nc.scalar, ag_in[0:1, tb * 512:(tb + 1) * 512], mrow[:], "sd")
                row_sd = p.dma(nc.scalar, ag_in[1:2, tb * 512:(tb + 1) * 512],
                               trow[:], "sd")
            p.wait(nc.gpsimd, "sd", p.c["sd"])
            cc_v = p.inc(nc.gpsimd.collective_compute(
                "AllGather", mybir.AluOpType.bypass,
                replica_groups=[list(range(NCORE))],
                ins=[ag_in[:].opt()], outs=[stats_g[:].opt()]), "cc", 1)
            p.wait(nc.gpsimd, "cc", cc_v)
            barrier(nc, p)

        with ExitStack() as ph:
            Eh = ph.enter_context
            xts = [Eh(nc.sbuf_tensor(f"a1x{i}", [128, 512], F32)) for i in range(2)]
            w_h, w_l = {}, {}
            for nm in ("q", "k", "v"):
                w_h[nm] = Eh(nc.sbuf_tensor(f"a1w{nm}h", [128, NDC * 256], BF16))
                w_l[nm] = Eh(nc.sbuf_tensor(f"a1w{nm}l", [128, NDC * 256], BF16))
            srow = Eh(nc.sbuf_tensor("a1sr", [1, 512], F32))
            mrow = Eh(nc.sbuf_tensor("a1mr", [1, 512], F32))
            r_bb = Eh(nc.sbuf_tensor("a1rb", [128, 512], F32))
            m_bb = Eh(nc.sbuf_tensor("a1mb", [128, 512], F32))
            lnf = [Eh(nc.sbuf_tensor(f"a1ln{i}", [128, 512], F32)) for i in range(2)]
            lnh = [Eh(nc.sbuf_tensor(f"a1lh{i}", [128, 512], BF16)) for i in range(2)]
            lnl = [Eh(nc.sbuf_tensor(f"a1ll{i}", [128, 512], BF16)) for i in range(2)]
            qk_st = [Eh(nc.sbuf_tensor(f"a1s{i}", [128, 512], BF16)) for i in range(8)]
            v_st = [Eh(nc.sbuf_tensor(f"a1v{i}", [128, 1024], BF16)) for i in range(2)]
            ps_qk = [Eh(nc.psum_tensor(f"a1p{i}", [128, 512], F32)) for i in range(4)]
            ps_v = [Eh(nc.psum_tensor(f"a1pv{i}", [128, 512], F32)) for i in range(2)]
            ps_bc = Eh(nc.psum_tensor("a1pb", [128, 512], F32))

            with nc.sbuf_tensor("a1wf", [128, NDC * 256], F32) as wf:
                for nm, w_ap in (("q", wq), ("k", wk), ("v", wv)):
                    v = p.dma(nc.sync, wf[:].rearrange("p (c m) -> p c m", m=256),
                              w_ap.rearrange("(c p) m -> p c m", p=128))
                    p.wait(nc.scalar, "ld", v)
                    split_hi_lo(nc, p, wf[:], w_h[nm][:], w_l[nm][:])
                    p.wait(nc.sync, "vc", p.c["vc"])

            ln_done = [0] * NTB
            dc_tm = {}
            dc_vc = {}
            qkv_sd = 0
            bc_tm = 0
            for tb in range(NTB):
                p.wait(nc.sync, "tm", bc_tm)
                gb, hb = tb // 2, (tb % 2) * 512
                p.dma(nc.sync, srow[:], stats_g[2 * gb:2 * gb + 1, hb:hb + 512], "lq")
                v = p.dma(nc.sync, mrow[:],
                          stats_g[2 * gb + 1:2 * gb + 2, hb:hb + 512], "lq")
                p.wait(nc.tensor, "lq", v)
                if tb >= 1:
                    p.wait(nc.tensor, "vc", ln_done[tb - 1])
                t1 = p.inc(nc.tensor.matmul(ps_bc[:], ones_r[:], srow[:],
                                            start=True, stop=True), "tm")
                p.wait(nc.vector, "tm", t1)
                p.v(lambda: nc.vector.tensor_copy(out=r_bb[:], in_=ps_bc[:]))
                p.wait(nc.tensor, "vc", p.c["vc"])
                t2 = p.inc(nc.tensor.matmul(ps_bc[:], ones_r[:], mrow[:],
                                            start=True, stop=True), "tm")
                bc_tm = t2
                p.wait(nc.vector, "tm", t2)
                p.v(lambda: nc.vector.tensor_copy(out=m_bb[:], in_=ps_bc[:]))

                for dc in range(NDC):
                    li = dc % 2
                    it = tb * NDC + dc
                    if it >= 2:
                        p.wait(nc.sync, "vc", dc_vc[it - 2])
                    ldv = p.dma(nc.sync, xts[li][:],
                                xT[dc * 128:(dc + 1) * 128, tb * 512:(tb + 1) * 512],
                                f"lx{li}")
                    xin = xts[li][:]
                    p.wait(nc.vector, f"lx{li}", ldv)
                    if it >= 2:
                        p.wait(nc.vector, "tm", dc_tm[it - 2])
                        p.wait(nc.scalar, "tm", dc_tm[it - 2])
                    p.v(lambda: nc.vector.tensor_tensor(out=lnf[li][:], in0=xin, in1=r_bb[:],
                                                  op=ALU.mult))
                    p.v(lambda: nc.vector.tensor_tensor(out=lnf[li][:], in0=lnf[li][:],
                                                  in1=m_bb[:], op=ALU.subtract))
                    dc_vc[it] = p.c["vc"]
                    split_hi_lo(nc, p, lnf[li][:], lnh[li][:], lnl[li][:])
                    p.wait(nc.tensor, "vc")
                    p.wait(nc.tensor, "sa")
                    first, last = dc == 0, dc == NDC - 1
                    for ti, nm in enumerate(("q", "k")):
                        for h in range(HPC):
                            lsl = slice(dc * 256 + h * 128, dc * 256 + (h + 1) * 128)
                            mm3(nc, p, ps_qk[ti * 2 + h][:], w_h[nm][:, lsl],
                                w_l[nm][:, lsl], lnh[li][:], lnl[li][:], first, last)
                    wsl = slice(dc * 256, (dc + 1) * 256)
                    for sub in range(4):
                        ssl = slice(sub * 128, (sub + 1) * 128)
                        pv = ps_v[sub // 2][:, (sub % 2) * 256:(sub % 2 + 1) * 256]
                        st0 = first and (sub % 2 == 0)
                        sp0 = last and (sub % 2 == 1)
                        p.inc(nc.tensor.matmul(pv, lnh[li][:, ssl], w_h["v"][:, wsl],
                                               start=st0, stop=False), "tm")
                        p.inc(nc.tensor.matmul(pv, lnl[li][:, ssl], w_h["v"][:, wsl],
                                               start=False, stop=False), "tm")
                        p.inc(nc.tensor.matmul(pv, lnh[li][:, ssl], w_l["v"][:, wsl],
                                               start=False, stop=sp0), "tm")
                    dc_tm[it] = p.c["tm"]
                ln_done[tb] = p.c["vc"]
                tm_qkv = p.c["tm"]

                p.wait(nc.scalar, "tm", tm_qkv)
                p.wait(nc.vector, "tm", tm_qkv)
                if tb >= 1:
                    p.wait(nc.scalar, "sd", qkv_sd)
                    p.wait(nc.vector, "sd", qkv_sd)
                for i in range(4):
                    split_hi_lo(nc, p, ps_qk[i][:], qk_st[2 * i][:], qk_st[2 * i + 1][:])
                for i in range(2):
                    split_hi_lo(nc, p, ps_v[i][:], v_st[i][:, 0:512],
                                v_st[i][:, 512:1024])
                p.wait(nc.scalar, "vc")
                tsl = slice(tb * 512, (tb + 1) * 512)
                dmap = [(qhi_d, 0, 0), (qlo_d, 0, 1), (qhi_d, 1, 2), (qlo_d, 1, 3),
                        (khi_d, 0, 4), (klo_d, 0, 5), (khi_d, 1, 6), (klo_d, 1, 7)]
                for dten, h, sti in dmap:
                    p.dma(nc.scalar, dten[h, :, tsl], qk_st[sti][:], "sd")
                for i in range(2):
                    for half, dten in ((0, vhi_d), (1, vlo_d)):
                        for h in range(HPC):
                            src = v_st[i][:].rearrange("p (a s h d) -> p a s h d",
                                                       a=2, s=2, h=2)[:, half, :, h, :]
                            k0 = tb * 4 + 2 * i
                            dst = dten[h, :, k0 * 128:(k0 + 2) * 128].rearrange(
                                "p (s d) -> p s d", d=128)
                            p.dma(nc.scalar, dst, src, "sd")
                qkv_sd = p.c["sd"]
            p.wait(nc.gpsimd, "sd", p.c["sd"])
            barrier(nc, p)

        with ExitStack() as ph:
            Eh = ph.enter_context
            k_hi = Eh(nc.sbuf_tensor("a2kh", [128, S], BF16))
            k_lo = Eh(nc.sbuf_tensor("a2kl", [128, S], BF16))
            vv_hi = Eh(nc.sbuf_tensor("a2vh", [128, S], BF16))
            vv_lo = Eh(nc.sbuf_tensor("a2vl", [128, S], BF16))
            q_hi = [Eh(nc.sbuf_tensor(f"a2qh{i}", [128, 512], BF16)) for i in range(2)]
            q_lo = [Eh(nc.sbuf_tensor(f"a2ql{i}", [128, 512], BF16)) for i in range(2)]
            pf = [Eh(nc.sbuf_tensor(f"a2pf{i}", [128, 512], F32)) for i in range(4)]
            p_hi = [Eh(nc.sbuf_tensor(f"a2ph{i}", [128, 512], BF16)) for i in range(4)]
            p_lo = [Eh(nc.sbuf_tensor(f"a2pl{i}", [128, 512], BF16)) for i in range(4)]
            zrow = Eh(nc.sbuf_tensor("a2z", [1, 512], F32))
            rzrow = Eh(nc.sbuf_tensor("a2rz", [1, 512], F32))
            rz_bb = Eh(nc.sbuf_tensor("a2rzb", [128, 512], F32))
            colv = Eh(nc.sbuf_tensor("a2cv", [128, 1], F32))
            otf = Eh(nc.sbuf_tensor("a2ot", [128, 512], F32))
            ps_st = [Eh(nc.psum_tensor(f"a2ps{i}", [128, 512], F32)) for i in range(4)]
            ps_ot = Eh(nc.psum_tensor("a2po", [128, 512], F32))
            ps_z = Eh(nc.psum_tensor("a2pz", [1, 512], F32))

            NQG = 16
            gq_tm = {}
            gq = 0
            kv_tm = 0
            for h in range(HPC):
                p.wait(nc.sync, "tm", kv_tm)
                p.dma(nc.sync, k_hi[:], khi_d[h], "lkv")
                p.dma(nc.sync, k_lo[:], klo_d[h], "lkv")
                p.dma(nc.sync, vv_hi[:], vhi_d[h], "lkv")
                lk = p.dma(nc.sync, vv_lo[:], vlo_d[h], "lkv")
                p.wait(nc.tensor, "lkv", lk)

                for kt in range(NT):
                    p.inc(nc.tensor.matmul(ps_st[0][:, 0:1],
                                           vv_hi[:, kt * 128:(kt + 1) * 128], ones_c[:],
                                           start=(kt == 0), stop=False), "tm")
                    p.inc(nc.tensor.matmul(ps_st[0][:, 0:1],
                                           vv_lo[:, kt * 128:(kt + 1) * 128], ones_c[:],
                                           start=False, stop=(kt == NT - 1)), "tm")
                p.wait(nc.vector, "tm", p.c["tm"])
                p.v(lambda: nc.vector.tensor_copy(out=colv[:], in_=ps_st[0][:, 0:1]))
                p.wait(nc.tensor, "vc", p.c["vc"])

                for qg in range(NQG):
                    qb = gq % 2
                    qsl = slice(qg * 512, (qg + 1) * 512)
                    if gq >= 2:
                        p.wait(nc.sync, "tm", gq_tm[gq - 2])
                    p.dma(nc.sync, q_hi[qb][:], qhi_d[h, :, qsl], f"lq{qb}")
                    lq2 = p.dma(nc.sync, q_lo[qb][:], qlo_d[h, :, qsl], f"lq{qb}")
                    p.wait(nc.tensor, f"lq{qb}", lq2)

                    kts = list(range(4 * qg, NT))
                    exp_done, split_done, vz_done = {}, {}, {}
                    for i, kt in enumerate(kts):
                        ncols = 128 * (min(kt, 4 * qg + 3) - 4 * qg + 1)
                        sb = i % 4
                        ksl = slice(kt * 128, (kt + 1) * 128)
                        if i >= 4:
                            p.wait(nc.tensor, "sa", exp_done[i - 4])
                        mm3(nc, p, ps_st[sb][:, 0:ncols], k_hi[:, ksl], k_lo[:, ksl],
                            q_hi[qb][:, 0:ncols], q_lo[qb][:, 0:ncols], True, True)
                        sc_tm = p.c["tm"]
                        p.wait(nc.scalar, "tm", sc_tm)
                        if i >= 4:
                            p.wait(nc.scalar, "vc", split_done[i - 4])
                            p.wait(nc.scalar, "tm", vz_done[i - 4])
                        sa = p.s(lambda: nc.scalar.activation(out=pf[sb][:, 0:ncols],
                                                        in_=ps_st[sb][:, 0:ncols],
                                                        func=AF.Exp, bias=expbc[:],
                                                        scale=sclc[:]))
                        exp_done[i] = sa
                        p.wait(nc.vector, "sa", sa)
                        if i >= 4:
                            p.wait(nc.vector, "tm", vz_done[i - 4])
                        if kt <= 4 * qg + 3:
                            p.v(lambda: nc.vector.tensor_tensor(
                                out=pf[sb][:, ncols - 128:ncols],
                                in0=pf[sb][:, ncols - 128:ncols],
                                in1=mask_s[:], op=ALU.mult))
                        p.v(lambda: nc.vector.tensor_copy(out=p_hi[sb][:, 0:ncols],
                                                    in_=pf[sb][:, 0:ncols]))
                        p.v(lambda: nc.vector.tensor_tensor(out=p_lo[sb][:, 0:ncols],
                                                      in0=pf[sb][:, 0:ncols],
                                                      in1=p_hi[sb][:, 0:ncols],
                                                      op=ALU.subtract))
                        split_done[i] = p.c["vc"]
                        p.wait(nc.tensor, "vc", split_done[i])
                        fin = kt == NT - 1
                        cuts = {0, ncols}
                        if kt <= 4 * qg + 3 and ncols - 128 > 0:
                            cuts.add(ncols - 128)
                        cuts = sorted(cuts)
                        segs = list(zip(cuts[:-1], cuts[1:]))
                        for c0, c1 in segs:
                            stt = c0 == 0 and kt == 4 * qg
                            seg_fin = fin and c1 == ncols
                            p.inc(nc.tensor.matmul(ps_ot[:, c0:c1], vv_hi[:, ksl],
                                                   p_hi[sb][:, c0:c1],
                                                   start=stt, stop=False), "tm")
                            p.inc(nc.tensor.matmul(ps_ot[:, c0:c1], vv_lo[:, ksl],
                                                   p_hi[sb][:, c0:c1],
                                                   start=False, stop=False), "tm")
                            p.inc(nc.tensor.matmul(ps_ot[:, c0:c1], vv_hi[:, ksl],
                                                   p_lo[sb][:, c0:c1],
                                                   start=False, stop=seg_fin), "tm")
                            p.inc(nc.tensor.matmul(ps_z[:, c0:c1], ones_c[:],
                                                   p_hi[sb][:, c0:c1],
                                                   start=stt, stop=False), "tm")
                            p.inc(nc.tensor.matmul(ps_z[:, c0:c1], ones_c[:],
                                                   p_lo[sb][:, c0:c1],
                                                   start=False, stop=seg_fin), "tm")
                        vz_done[i] = p.c["tm"]
                    gq_tm[gq] = p.c["tm"]
                    av_tm = p.c["tm"]

                    p.wait(nc.vector, "tm", av_tm)
                    p.v(lambda: nc.vector.tensor_copy(out=zrow[:], in_=ps_z[:]))
                    if qg == NQG - 1:
                        p.v(lambda: nc.vector.memset(zrow[:, 511:512], 1.0))
                    p.v(lambda: nc.vector.reciprocal(out=rzrow[:], in_=zrow[:]))
                    p.wait(nc.tensor, "vc", p.c["vc"])
                    t2 = p.inc(nc.tensor.matmul(ps_st[0][:, 0:512], ones_r[:],
                                                rzrow[:, 0:512], start=True,
                                                stop=True), "tm")
                    p.wait(nc.vector, "tm", t2)
                    p.v(lambda: nc.vector.tensor_copy(out=rz_bb[:], in_=ps_st[0][:]))
                    p.v(lambda: nc.vector.tensor_tensor(out=otf[:], in0=ps_ot[:],
                                                  in1=rz_bb[:], op=ALU.mult))
                    if qg == NQG - 1:
                        p.v(lambda: nc.vector.tensor_scalar_mul(otf[:, 511:512], colv[:],
                                                          1.0 / S))
                    split_hi_lo(nc, p, otf[:], o_hi[h][:, qsl], o_lo[h][:, qsl])
                    p.wait(nc.tensor, "vc", p.c["vc"])
                    gq += 1
                kv_tm = p.c["tm"]
            barrier(nc, p)
        p.vdrain = True

        with ExitStack() as ph:
            Eh = ph.enter_context
            wof = Eh(nc.sbuf_tensor("a3wf", [128, HPC * DM], F32))
            wo_h = Eh(nc.sbuf_tensor("a3wh", [128, HPC * DM], BF16))
            wo_l = Eh(nc.sbuf_tensor("a3wl", [128, HPC * DM], BF16))
            ost = [Eh(nc.sbuf_tensor(f"a3o{i}", [128, 1024], F32)) for i in range(2)]
            ps_o = [Eh(nc.psum_tensor(f"a3p{i}", [128, 1024], F32)) for i in range(2)]

            v = p.dma(nc.sync, wof[:].rearrange("p (h n) -> p h n", h=HPC),
                      wo.rearrange("(h p) n -> p h n", p=128))
            p.wait(nc.scalar, "ld", v)
            split_hi_lo(nc, p, wof[:], wo_h[:], wo_l[:])
            p.wait(nc.tensor, "vc", p.c["vc"])
            p.wait(nc.tensor, "sa", p.c["sa"])

            it_sa = {}
            it_sd = {}
            cc_at = {}
            it = 0
            for g in range(4):
                for d2 in range(g * 4, g * 4 + 4):
                    for tc in range(NCORE):
                        tsl = slice(tc * 1024, (tc + 1) * 1024)
                        i = it % 2
                        if it >= 2:
                            p.wait(nc.tensor, "sa", it_sa[it - 2])
                        for c0 in (0, 512):
                            csl = slice(c0, c0 + 512)
                            for hh in range(HPC):
                                lsl = slice(hh * DM + d2 * 128,
                                            hh * DM + (d2 + 1) * 128)
                                mm3(nc, p, ps_o[i][:, csl], wo_h[:, lsl],
                                    wo_l[:, lsl], o_hi[hh][:, tsl][:, csl],
                                    o_lo[hh][:, tsl][:, csl],
                                    start=(hh == 0), stop=(hh == HPC - 1))
                        tmv = p.c["tm"]
                        p.wait(nc.scalar, "tm", tmv)
                        if it >= 2:
                            p.wait(nc.scalar, f"so{i}", it_sd[it - 2])
                        it_sa[it] = p.s(lambda: nc.scalar.activation(
                            out=ost[i][:], in_=ps_o[i][:], func=AF.Copy))
                        it_sd[it] = p.dma(
                            nc.scalar,
                            rs_g[g][tc, (d2 % 4) * 128:(d2 % 4 + 1) * 128, :],
                            ost[i][:], f"so{i}")
                        it += 1
                p.wait(nc.gpsimd, "so0", p.c["so0"])
                p.wait(nc.gpsimd, "so1", p.c["so1"])
                cc_at[g] = p.inc(nc.gpsimd.collective_compute(
                    "ReduceScatter", mybir.AluOpType.add,
                    replica_groups=[list(range(NCORE))],
                    ins=[rs_g[g][:].opt()], outs=[rso_g[g][:].opt()]), "cc", 1)
        ascope.close()
        barrier(nc, p)
        xnewT = E(nc.sbuf_tensor("xnewT", [128, NDC * TPC], F32))
        rstd = E(nc.sbuf_tensor("b1rs", [1, TPC], F32))
        mr = E(nc.sbuf_tensor("b1mrr", [1, TPC], F32))

        with ExitStack() as ph:
            Eh = ph.enter_context
            pin = [Eh(nc.sbuf_tensor(f"b0p{i}", [128, TPC], F32)) for i in range(4)]
            done_vc = {}
            for dc in range(NDC):
                dsl = slice((dc % 4) * 128, (dc % 4 + 1) * 128)
                xsl = slice(dc * TPC, (dc + 1) * TPC)
                i0, i1 = (dc % 2) * 2, (dc % 2) * 2 + 1
                if dc >= 2:
                    p.wait(nc.sync, "vc", done_vc[dc - 2])
                p.dma(nc.sync, pin[i0][:], rso_g[dc // 4][dsl, :], f"lp{i0}")
                v1 = p.dma(nc.sync, pin[i1][:], xTm[dc * 128:(dc + 1) * 128, :],
                           f"lp{i1}")
                p.wait(nc.vector, f"lp{i0}", p.c[f"lp{i0}"])
                p.wait(nc.vector, f"lp{i1}", v1)
                p.v(lambda: nc.vector.tensor_tensor(out=xnewT[:, xsl], in0=pin[i0][:],
                                                    in1=pin[i1][:], op=ALU.add))
                done_vc[dc] = p.c["vc"]
            barrier(nc, p)
        with ExitStack() as ph:
            Eh = ph.enter_context
            xbf = [Eh(nc.sbuf_tensor(f"b1xb{i}", [128, TPC], BF16)) for i in range(2)]
            sqb = [Eh(nc.sbuf_tensor(f"b1sq{i}", [128, TPC], BF16)) for i in range(2)]
            trow = Eh(nc.sbuf_tensor("b1t1", [1, TPC], F32))
            trow2 = Eh(nc.sbuf_tensor("b1t2", [1, TPC], F32))
            rrow = Eh(nc.sbuf_tensor("b1r", [1, TPC], F32))
            ps_s = Eh(nc.psum_tensor("b1ps", [1, TPC], F32))
            ps_q = Eh(nc.psum_tensor("b1pq", [1, TPC], F32))
            ones_cb = Eh(nc.sbuf_tensor("b1oc", [128, 1], BF16))
            p.v(lambda: nc.vector.memset(ones_cb[:], 1.0))
            oc_vc = p.c["vc"]

            it_tm = {}
            p.wait(nc.tensor, "vc", oc_vc)
            for dc in range(NDC):
                i = dc % 2
                xin = xnewT[:, dc * TPC:(dc + 1) * TPC]
                if dc >= 2:
                    p.wait(nc.scalar, "tm", it_tm[dc - 2])
                p.s(lambda: nc.scalar.activation(out=xbf[i][:], in_=xin, func=AF.Copy))
                sa = p.s(lambda: nc.scalar.activation(out=sqb[i][:], in_=xin,
                                                func=AF.Square))
                p.wait(nc.tensor, "sa", sa)
                for c0 in (0, 512):
                    csl = slice(c0, c0 + 512)
                    p.inc(nc.tensor.matmul(ps_s[:, csl], ones_cb[:], xbf[i][:, csl],
                                           start=(dc == 0), stop=(dc == NDC - 1)), "tm")
                    p.inc(nc.tensor.matmul(ps_q[:, csl], ones_cb[:], sqb[i][:, csl],
                                           start=(dc == 0), stop=(dc == NDC - 1)), "tm")
                it_tm[dc] = p.c["tm"]
            p.wait(nc.vector, "tm", p.c["tm"])
            ln_rows(nc, p, ps_s[:], ps_q[:], trow[:], trow2[:], rrow[:],
                    rstd[:], mr[:], eps1[:])
            if debug:
                p.wait(nc.scalar, "vc")
                p.dma(nc.scalar, d_rm[0:1, :], rstd[:], "sd")
                p.dma(nc.scalar, d_rm[1:2, :], mr[:], "sd")
                p.dma(nc.scalar, d_xnew[:], xnewT[:], "sd")
            barrier(nc, p)

        TP2 = 512
        p.vdrain = False
        with ExitStack() as ph:
            Eh = ph.enter_context
            ln_h2 = Eh(nc.sbuf_tensor("b4lnh", [128, NDC * TP2], BF16))
            ln_l2 = Eh(nc.sbuf_tensor("b4lnl", [128, NDC * TP2], BF16))
            lnf2 = [Eh(nc.sbuf_tensor(f"b4lf{i}", [128, TP2], F32)) for i in range(2)]
            r_bb = Eh(nc.sbuf_tensor("b4rb", [128, TP2], F32))
            m_bb = Eh(nc.sbuf_tensor("b4mb", [128, TP2], F32))
            ust = [Eh(nc.sbuf_tensor(f"b4us{i}", [128, DM], F32)) for i in range(2)]
            up_h = [[Eh(nc.sbuf_tensor(f"b4u{k}h{b}", [128, DM], BF16))
                     for b in range(2)] for k in range(2)]
            up_l = [[Eh(nc.sbuf_tensor(f"b4u{k}l{b}", [128, DM], BF16))
                     for b in range(2)] for k in range(2)]
            dn_h = [Eh(nc.sbuf_tensor(f"b4dh{i}", [128, FE], BF16)) for i in range(2)]
            dn_l = [Eh(nc.sbuf_tensor(f"b4dl{i}", [128, FE], BF16)) for i in range(2)]
            h_hi = Eh(nc.sbuf_tensor("b4hh", [128, 11 * TP2], BF16))
            h_lo = Eh(nc.sbuf_tensor("b4hl", [128, 11 * TP2], BF16))
            hf = Eh(nc.sbuf_tensor("b4hf", [128, TP2], F32))
            w_bb = Eh(nc.sbuf_tensor("b4wb", [128, TP2], F32))
            wrow = Eh(nc.sbuf_tensor("b4wro", [1, NEXP * TP2], F32))
            wr_h = Eh(nc.sbuf_tensor("b4wrh", [128, NDC * 8], BF16))
            wr_l = Eh(nc.sbuf_tensor("b4wrl", [128, NDC * 8], BF16))
            lg8 = Eh(nc.sbuf_tensor("b4lg8", [8, TP2], F32))
            ltok = Eh(nc.sbuf_tensor("b4lt", [128, 4 * NEXP], F32))
            m1 = Eh(nc.sbuf_tensor("b4m1", [128, 4], F32))
            zr = Eh(nc.sbuf_tensor("b4zr", [128, 4], F32))
            prob = Eh(nc.sbuf_tensor("b4p", [128, 4 * NEXP], F32))
            tmp = Eh(nc.sbuf_tensor("b4tm2", [128, 4 * NEXP], F32))
            wtok = Eh(nc.sbuf_tensor("b4wt", [128, 4 * NEXP], F32))
            ps_g = [Eh(nc.psum_tensor(f"b4pg{i}", [128, TP2], F32)) for i in range(2)]
            ps_u = [Eh(nc.psum_tensor(f"b4pu{i}", [128, TP2], F32)) for i in range(2)]
            ps_o = [Eh(nc.psum_tensor(f"b4po{i}", [128, TP2], F32)) for i in range(2)]
            ps_wb = Eh(nc.psum_tensor("b4pw", [128, TP2], F32))
            ps_r8 = Eh(nc.psum_tensor("b4pr", [8, TP2], F32))

            v = p.dma(nc.sync, ust[0][:, 0:NDC * 8].rearrange("p (c e) -> p c e", e=8),
                      wrT.rearrange("(c p) e -> p c e", p=128), "lw")
            p.wait(nc.scalar, "lw", v)
            split_w(nc, p, ust[0][:, 0:NDC * 8], wr_h[:], wr_l[:])

            stq = [0]
            stage_free = [p.c["vc"], p.c["vc"]]
            up_used_tm = [0, 0]
            up_split_vc = [0, 0]
            up_split_sa = [0, 0]
            silu_sa = [0, 0]
            psu_read_vc = [0, 0]
            dn_used_tm = [0, 0]
            dn_split_vc = [0, 0]
            dn_split_sa = [0, 0]
            xadd_vc = [0, 0]
            wbb_read_vc = [0]
            pswb_read = [0]
            psr_free = [0]
            h_done_vc = [0]
            h_done_sa = [0]
            upi = [0]
            dni = [0]
            item_ld = {}
            dbg_h = [False]
            dbg_d = [False]
            dbg_x2 = [False]

            def up_srcs(task, fc):
                kind, arg = task
                if kind == "s":
                    gfc = arg + fc
                    bases = (wsgT, wsuT)
                    return [b.rearrange("(c p) f -> p c f", p=128)[
                        :, :, gfc * 128:(gfc + 1) * 128] for b in bases]
                return [b[arg].rearrange("(c p) f -> p c f", p=128)[
                    :, :, fc * 128:(fc + 1) * 128] for b in (wgT, wuT)]

            def emit_loads(key, item):
                if item[0] == "up":
                    _, task, fc, b, we = item
                    lands, stages = [], []
                    for src in up_srcs(task, fc):
                        s = stq[0] % 2
                        stq[0] += 1
                        p.wait(nc.sync, "vc", stage_free[s])
                        lands.append(p.dma(
                            nc.sync,
                            ust[s][:].rearrange("p (c f) -> p c f", f=128),
                            src, "lw"))
                        stages.append(s)
                    item_ld[key] = (stages, lands)
                elif item[0] == "down":
                    _, task, dc, b, th = item
                    kind, arg = task
                    s = stq[0] % 2
                    stq[0] += 1
                    dsl = slice(dc * 128, (dc + 1) * 128)
                    if kind == "s":
                        src = wsdT.rearrange("(c p) d -> p c d", p=128)[
                            :, arg:arg + 11, dsl]
                    else:
                        src = wdT[arg].rearrange("(c p) d -> p c d", p=128)[:, :, dsl]
                    p.wait(nc.sync, "vc", stage_free[s])
                    land = p.dma(nc.sync,
                                 ust[s][:, 0:FE].rearrange("p (c d) -> p c d", d=128),
                                 src, "lg")
                    item_ld[key] = ([s], [land])

            def emit_splits(key, item):
                if item[0] == "up":
                    _, task, fc, b, we = item
                    stages, lands = item_ld[key]
                    p.wait(nc.scalar, "tm", up_used_tm[b])
                    for j in range(2):
                        s = stages[j]
                        p.wait(nc.scalar, "lw", lands[j])
                        split_w(nc, p, ust[s][:], up_h[j][b][:], up_l[j][b][:])
                        stage_free[s] = p.c["vc"]
                    up_split_vc[b] = p.c["vc"]
                    up_split_sa[b] = p.c["sa"]
                elif item[0] == "down":
                    _, task, dc, b, th = item
                    (s,), (land,) = item_ld[key]
                    p.wait(nc.scalar, "lg", land)
                    p.wait(nc.scalar, "tm", dn_used_tm[b])
                    split_w(nc, p, ust[s][:, 0:FE], dn_h[b][:], dn_l[b][:])
                    stage_free[s] = p.c["vc"]
                    dn_split_vc[b] = p.c["vc"]
                    dn_split_sa[b] = p.c["sa"]

            def emit_compute(item):
                if item[0] == "up":
                    _, task, fc, b, we = item
                    p.wait(nc.tensor, "vc", up_split_vc[b])
                    p.wait(nc.tensor, "sa", up_split_sa[b])
                    p.wait(nc.tensor, "sa", silu_sa[b])
                    p.wait(nc.tensor, "vc", psu_read_vc[b])
                    for dc in range(NDC):
                        wsl = slice(dc * 128, (dc + 1) * 128)
                        lsl = slice(dc * TP2, (dc + 1) * TP2)
                        first, last = dc == 0, dc == NDC - 1
                        mm3(nc, p, ps_g[b][:], up_h[0][b][:, wsl], up_l[0][b][:, wsl],
                            ln_h2[:, lsl], ln_l2[:, lsl], first, last)
                        mm3(nc, p, ps_u[b][:], up_h[1][b][:, wsl], up_l[1][b][:, wsl],
                            ln_h2[:, lsl], ln_l2[:, lsl], first, last)
                    up_used_tm[b] = p.c["tm"]
                    utm = p.c["tm"]
                    p.wait(nc.scalar, "tm", utm)
                    p.wait(nc.scalar, "vc", h_done_vc[0])
                    sa = p.s(lambda: nc.scalar.activation(out=hf[:], in_=ps_g[b][:],
                                                    func=AF.Silu))
                    silu_sa[b] = sa
                    p.wait(nc.vector, "sa", sa)
                    p.wait(nc.vector, "tm", utm)
                    p.v(lambda: nc.vector.tensor_tensor(out=hf[:], in0=hf[:],
                                                  in1=ps_u[b][:], op=ALU.mult))
                    if we is not None:
                        p.v(lambda: nc.vector.tensor_tensor(out=hf[:], in0=hf[:],
                                                      in1=w_bb[:], op=ALU.mult))
                        wbb_read_vc[0] = p.c["vc"]
                    psu_read_vc[b] = p.c["vc"]
                    hsl = slice(fc * TP2, (fc + 1) * TP2)
                    split_hi_lo(nc, p, hf[:], h_hi[:, hsl], h_lo[:, hsl])
                    h_done_vc[0] = p.c["vc"]
                    h_done_sa[0] = p.c["sa"]
                    if debug and task == ("s", 0) and fc == 10 and not dbg_h[0]:
                        dbg_h[0] = True
                        p.wait(nc.scalar, "vc")
                        p.dma(nc.scalar, d_hh[:], h_hi[:], "sd")
                        p.dma(nc.scalar, d_hl[:], h_lo[:], "sd")
                        p.wait(nc.scalar, "sd", p.c["sd"])
                elif item[0] == "down":
                    _, task, dc, b, th = item
                    p.wait(nc.tensor, "vc", dn_split_vc[b])
                    p.wait(nc.tensor, "sa", dn_split_sa[b])
                    p.wait(nc.tensor, "vc", xadd_vc[b])
                    if dc == 0:
                        p.wait(nc.tensor, "vc", h_done_vc[0])
                        p.wait(nc.tensor, "sa", h_done_sa[0])
                    for fc in range(11):
                        fsl = slice(fc * TP2, (fc + 1) * TP2)
                        mm3(nc, p, ps_o[b][:], dn_h[b][:, fc * 128:(fc + 1) * 128],
                            dn_l[b][:, fc * 128:(fc + 1) * 128],
                            h_hi[:, fsl], h_lo[:, fsl], fc == 0, fc == 10)
                    dn_used_tm[b] = p.c["tm"]
                    dtm = p.c["tm"]
                    p.wait(nc.vector, "tm", dtm)
                    xsl = slice(dc * TPC + th * TP2, dc * TPC + th * TP2 + TP2)
                    p.v(lambda: nc.vector.tensor_tensor(out=xnewT[:, xsl],
                                                  in0=xnewT[:, xsl],
                                                  in1=ps_o[b][:], op=ALU.add))
                    xadd_vc[b] = p.c["vc"]
                    if debug and task == ("s", 0) and dc == 1 and not dbg_d[0]:
                        dbg_d[0] = True
                        p.wait(nc.scalar, "sa", dn_split_sa[b])
                        p.wait(nc.scalar, "vc", dn_split_vc[b])
                        p.dma(nc.scalar, d_dwh[:], dn_h[b][:], "sd")
                        p.dma(nc.scalar, d_dwl[:], dn_l[b][:], "sd")
                        p.wait(nc.scalar, "sd", p.c["sd"])
                    if debug and task == ("s", 0) and dc == NDC - 1 and not dbg_x2[0]:
                        dbg_x2[0] = True
                        p.wait(nc.scalar, "vc")
                        p.dma(nc.scalar, d_x2[:], xnewT[:], "sd")
                        p.wait(nc.scalar, "sd", p.c["sd"])
                else:
                    _, e = item[0], item[1]
                    p.wait(nc.tensor, "vc", wbb_read_vc[0])
                    p.wait(nc.tensor, "vc", pswb_read[0])
                    t = p.inc(nc.tensor.matmul(ps_wb[:], ones_r[:],
                                               wrow[0:1, e * TP2:(e + 1) * TP2],
                                               start=True, stop=True), "tm")
                    p.wait(nc.vector, "tm", t)
                    p.v(lambda: nc.vector.tensor_copy(out=w_bb[:], in_=ps_wb[:]))
                    pswb_read[0] = p.c["vc"]

            def fill_ln_router(th):
                gate_tm = p.c["tm"]
                p.wait(nc.vector, "tm", gate_tm)
                p.wait(nc.scalar, "tm", gate_tm)
                p.wait(nc.tensor, "vc", pswb_read[0])
                tsl = slice(th * TP2, (th + 1) * TP2)
                t1 = p.inc(nc.tensor.matmul(ps_wb[:], ones_r[:], rstd[0:1, tsl],
                                            start=True, stop=True), "tm")
                p.wait(nc.vector, "tm", t1)
                p.v(lambda: nc.vector.tensor_copy(out=r_bb[:], in_=ps_wb[:]))
                p.wait(nc.tensor, "vc", p.c["vc"])
                t2 = p.inc(nc.tensor.matmul(ps_wb[:], ones_r[:], mr[0:1, tsl],
                                            start=True, stop=True), "tm")
                p.wait(nc.vector, "tm", t2)
                p.v(lambda: nc.vector.tensor_copy(out=m_bb[:], in_=ps_wb[:]))
                pswb_read[0] = p.c["vc"]
                p.wait(nc.tensor, "vc", psr_free[0])
                lnf_sa = {}
                for dc in range(NDC):
                    i = dc % 2
                    xin = xnewT[:, dc * TPC + th * TP2: dc * TPC + th * TP2 + TP2]
                    lsl = slice(dc * TP2, (dc + 1) * TP2)
                    if dc >= 2:
                        p.wait(nc.vector, "sa", lnf_sa[dc - 2])
                    p.v(lambda: nc.vector.tensor_tensor(out=lnf2[i][:], in0=xin,
                                                  in1=r_bb[:], op=ALU.mult))
                    p.v(lambda: nc.vector.tensor_tensor(out=lnf2[i][:], in0=lnf2[i][:],
                                                  in1=m_bb[:], op=ALU.subtract))
                    split_hi_lo(nc, p, lnf2[i][:], ln_h2[:, lsl], ln_l2[:, lsl])
                    lnf_sa[dc] = p.c["sa"]
                    p.wait(nc.tensor, "vc", p.c["vc"])
                    p.wait(nc.tensor, "sa", p.c["sa"])
                    mm3(nc, p, ps_r8[:], wr_h[:, dc * 8:(dc + 1) * 8],
                        wr_l[:, dc * 8:(dc + 1) * 8],
                        ln_h2[:, lsl], ln_l2[:, lsl], dc == 0, dc == NDC - 1)
                if debug and th == 0:
                    p.wait(nc.scalar, "vc")
                    p.dma(nc.scalar, d_lnh[:], ln_h2[:], "sd")
                    p.dma(nc.scalar, d_lnl[:], ln_l2[:], "sd")
                    p.wait(nc.scalar, "sd", p.c["sd"])

            def router_post(th):
                p.vdrain = True
                rtm = p.c["tm"]
                p.wait(nc.vector, "tm", rtm)
                p.v(lambda: nc.vector.tensor_copy(out=lg8[:], in_=ps_r8[:]))
                psr_free[0] = p.c["vc"]
                p.wait(nc.tensor, "vc", p.c["vc"])
                p.wait(nc.tensor, "vc", pswb_read[0])
                for tt in range(4):
                    p.inc(nc.tensor.transpose(ps_wb[:, tt * 8:(tt + 1) * 8],
                                              lg8[0:8, tt * 128:(tt + 1) * 128],
                                              ident_s[0:8, 0:8]), "tm")
                ttm = p.c["tm"]
                p.wait(nc.vector, "tm", ttm)
                p.v(lambda: nc.vector.tensor_copy(out=ltok[:, 0:4 * NEXP],
                                            in_=ps_wb[:, 0:4 * NEXP]))
                pswb_read[0] = p.c["vc"]
                for tt in range(4):
                    tsl = slice(tt * 8, (tt + 1) * 8)
                    msl = slice(tt, tt + 1)
                    p.v(lambda: nc.vector.tensor_reduce(out=m1[:, msl], in_=ltok[:, tsl],
                                                  axis=AXX, op=ALU.max))
                    p.v(lambda: nc.vector.tensor_scalar_mul(m1[:, msl], m1[:, msl], -1.0))
                    va = p.c["vc"]
                    p.wait(nc.scalar, "vc", va)
                    sa = p.s(lambda: nc.scalar.activation(out=prob[:, tsl],
                                                    in_=ltok[:, tsl],
                                                    func=AF.Exp, bias=m1[:, msl]))
                    p.wait(nc.vector, "sa", sa)
                    p.v(lambda: nc.vector.tensor_reduce(out=zr[:, msl], in_=prob[:, tsl],
                                                  axis=AXX, op=ALU.add))
                    p.v(lambda: nc.vector.reciprocal(out=zr[:, msl], in_=zr[:, msl]))
                    p.v(lambda: nc.vector.tensor_scalar(out=prob[:, tsl], in0=prob[:, tsl],
                                                  scalar1=zr[:, msl], scalar2=None,
                                                  op0=ALU.mult))
                    p.v(lambda: nc.vector.tensor_reduce(out=m1[:, msl], in_=prob[:, tsl],
                                                  axis=AXX, op=ALU.max))
                    p.v(lambda: nc.vector.tensor_scalar(out=tmp[:, tsl], in0=prob[:, tsl],
                                                  scalar1=m1[:, msl], scalar2=None,
                                                  op0=ALU.is_equal))
                    p.v(lambda: nc.vector.scalar_tensor_tensor(out=tmp[:, tsl],
                                                         in0=tmp[:, tsl],
                                                         scalar=-BIG, in1=prob[:, tsl],
                                                         op0=ALU.mult, op1=ALU.add))
                    p.v(lambda: nc.vector.tensor_reduce(out=m1[:, msl], in_=tmp[:, tsl],
                                                  axis=AXX, op=ALU.max))
                    p.v(lambda: nc.vector.tensor_scalar(out=tmp[:, tsl], in0=prob[:, tsl],
                                                  scalar1=m1[:, msl], scalar2=None,
                                                  op0=ALU.is_ge))
                    p.v(lambda: nc.vector.tensor_tensor(out=wtok[:, tsl], in0=prob[:, tsl],
                                                  in1=tmp[:, tsl], op=ALU.mult))
                p.wait(nc.tensor, "vc", p.c["vc"])
                p.wait(nc.tensor, "vc", pswb_read[0])
                for e in range(NEXP):
                    for tt in range(4):
                        p.inc(nc.tensor.transpose(
                            ps_wb[0:1, tt * 128:(tt + 1) * 128],
                            wtok[:, tt * 8 + e: tt * 8 + e + 1],
                            ident_s[:]), "tm")
                    te = p.c["tm"]
                    p.wait(nc.vector, "tm", te)
                    p.v(lambda: nc.vector.tensor_copy(
                        out=wrow[0:1, e * TP2:(e + 1) * TP2],
                        in_=ps_wb[0:1, 0:TP2]))
                    p.wait(nc.tensor, "vc", p.c["vc"])
                pswb_read[0] = p.c["vc"]
                p.vdrain = False

            tasks = [("s", 0), ("s", 11)] + [("e", e) for e in range(NEXP)]
            for th in range(2):
                items = []
                for task in tasks:
                    if task[0] == "e":
                        items.append(("wbb", task[1]))
                        we = 0
                    else:
                        we = None
                    for fc in range(11):
                        b = upi[0] % 2
                        upi[0] += 1
                        items.append(("up", task, fc, b, we))
                    for dc in range(NDC):
                        b = dni[0] % 2
                        dni[0] += 1
                        items.append(("down", task, dc, b, th))
                n = len(items)
                emit_loads((th, 0), items[0])
                emit_splits((th, 0), items[0])
                emit_loads((th, 1), items[1])
                fill_ln_router(th)
                router_post(th)
                for i in range(n):
                    if i + 1 < n:
                        emit_splits((th, i + 1), items[i + 1])
                    if i + 2 < n:
                        emit_loads((th, i + 2), items[i + 2])
                    emit_compute(items[i])
            if debug:
                p.wait(nc.scalar, "vc")
                p.dma(nc.scalar, d_x3[:], xnewT[:], "sd")
            barrier(nc, p)

        p.vdrain = True
        with ExitStack() as ph:
            Eh = ph.enter_context
            yst = [Eh(nc.sbuf_tensor(f"b5y{i}", [128, DM], F32)) for i in range(2)]
            ps_t = Eh(nc.psum_tensor("b5pt", [128, 128], F32))
            y_sd = [0, 0]
            for tt in range(8):
                i = tt % 2
                if tt >= 2:
                    p.wait(nc.vector, f"so{i}", y_sd[i])
                for dc in range(NDC):
                    t1 = p.inc(nc.tensor.transpose(
                        ps_t[:],
                        xnewT[:, dc * TPC + tt * 128: dc * TPC + (tt + 1) * 128],
                        ident_s[:]), "tm")
                    p.wait(nc.vector, "tm", t1)
                    p.v(lambda: nc.vector.tensor_copy(
                        out=yst[i][:, dc * 128:(dc + 1) * 128], in_=ps_t[:]))
                    p.wait(nc.tensor, "vc", p.c["vc"])
                p.wait(nc.scalar, "vc")
                y_sd[i] = p.dma(nc.scalar, y[tt * 128:(tt + 1) * 128, :], yst[i][:],
                                f"so{i}")
            p.wait(nc.sync, "so0", p.c["so0"])
            p.wait(nc.sync, "so1", p.c["so1"])
    return nc



def kernel(L0_attn_x, L0_attn_Wq, L0_attn_Wk, L0_attn_Wv, L0_attn_Wo,
           L0_ffn_Wr, L0_ffn_Wg, L0_ffn_Wu, L0_ffn_Wd,
           L0_ffn_Wsg, L0_ffn_Wsu, L0_ffn_Wsd):
    x = np.ascontiguousarray(np.asarray(L0_attn_x, dtype=np.float32)[0])
    xT = np.ascontiguousarray(x.T)
    Wq = np.asarray(L0_attn_Wq, dtype=np.float32)
    Wk = np.asarray(L0_attn_Wk, dtype=np.float32)
    Wv = np.asarray(L0_attn_Wv, dtype=np.float32)
    Wo = np.asarray(L0_attn_Wo, dtype=np.float32)

    mask01 = np.tril(np.ones((128, 128), dtype=np.float32), -1)
    identf = np.eye(128, dtype=np.float32)
    wgT = np.ascontiguousarray(np.asarray(L0_ffn_Wg, dtype=np.float32).transpose(0, 2, 1))
    wuT = np.ascontiguousarray(np.asarray(L0_ffn_Wu, dtype=np.float32).transpose(0, 2, 1))
    wdT = np.ascontiguousarray(np.asarray(L0_ffn_Wd, dtype=np.float32).transpose(0, 2, 1))
    wsgT = np.ascontiguousarray(np.asarray(L0_ffn_Wsg, dtype=np.float32).T)
    wsuT = np.ascontiguousarray(np.asarray(L0_ffn_Wsu, dtype=np.float32).T)
    wsdT = np.ascontiguousarray(np.asarray(L0_ffn_Wsd, dtype=np.float32).T)
    wrT = np.ascontiguousarray(np.asarray(L0_ffn_Wr, dtype=np.float32).T)

    ncF = build_full()
    in_maps = []
    for c in range(NCORE):
        csl = slice(c * HPC * HD, (c + 1) * HPC * HD)
        in_maps.append({
            "xT": xT,
            "wq": np.ascontiguousarray(Wq[:, csl]),
            "wk": np.ascontiguousarray(Wk[:, csl]),
            "wv": np.ascontiguousarray(Wv[:, csl]),
            "wo": np.ascontiguousarray(Wo[csl, :]),
            "mask01": mask01,
            "xTm": np.ascontiguousarray(xT[:, c * TPC:(c + 1) * TPC]),
            "wgT": wgT, "wuT": wuT, "wdT": wdT,
            "wsgT": wsgT, "wsuT": wsuT, "wsdT": wsdT, "wrT": wrT,
            "identf": identf,
        })
    res = run_bass_kernel_spmd(ncF, in_maps, core_ids=list(range(NCORE)))
    out = np.concatenate([res.results[c]["y"] for c in range(NCORE)], axis=0)
    return np.ascontiguousarray(out.reshape(1, S, DM), dtype=np.float32)


# version 19 (final)
# speedup vs baseline: 1.2725x; 1.2725x over previous
import math
from contextlib import ExitStack

import numpy as np
import concourse.bass as bass
import concourse.mybir as mybir
from concourse.bass_utils import run_bass_kernel_spmd

F32 = mybir.dt.float32
BF16 = mybir.dt.bfloat16
AF = mybir.ActivationFunctionType
ALU = mybir.AluOpType
AXX = mybir.AxisListType.X

S = 8192
DM = 2048
HD = 128
NCORE = 8
HPC = 2
TPC = S // NCORE
NDC = DM // 128
NT = S // 128
NTB = S // 512
FE = 1408
FS = 2816
NEXP = 8
EXPB = -8.0
SCL = 1.0 / math.sqrt(HD)
BIG = 1.0e30
EPS = 1e-5


class P:

    def __init__(self, nc, stack, names):
        self.nc = nc
        self.h = {}
        self.c = {}
        for n in names:
            self.h[n] = stack.enter_context(nc.semaphore(n))
            self.c[n] = 0

    def inc(self, instr, name, k=1):
        instr.then_inc(self.h[name], k)
        self.c[name] += k
        return self.c[name]

    def dma(self, eng, out, in_, name="ld"):
        return self.inc(eng.dma_start(out=out, in_=in_), name, 16)

    def wait(self, eng, name, val=None):
        eng.wait_ge(self.h[name], self.c[name] if val is None else val)

    vdrain = True

    def v(self, f):
        if self.vdrain:
            self.wait(self.nc.vector, "vc")
        return self.inc(f(), "vc")

    def s(self, f):
        return self.inc(f(), "sa")


def barrier(nc, p):
    engines = (nc.tensor, nc.vector, nc.scalar, nc.gpsimd, nc.sync)
    for nm in p.h:
        if nm in ("tm", "vc", "sa", "ph"):
            continue
        p.wait(nc.gpsimd, nm)
    for e in engines:
        p.inc(e.nop(), "ph", 1)
    for e in engines:
        p.wait(e, "ph")


def split_hi_lo(nc, p, src_ap, hi_ap, lo_ap):
    p.wait(nc.scalar, "vc")
    va = p.s(lambda: nc.scalar.activation(out=hi_ap, in_=src_ap, func=AF.Copy))
    p.wait(nc.vector, "sa", va)
    p.v(lambda: nc.vector.tensor_tensor(out=lo_ap, in0=src_ap, in1=hi_ap,
                                  op=ALU.subtract))


def split_w(nc, p, src_ap, hi_ap, lo_ap):
    va = p.s(lambda: nc.scalar.activation(out=hi_ap, in_=src_ap, func=AF.Copy))
    p.wait(nc.vector, "sa", va)
    p.v(lambda: nc.vector.tensor_tensor(out=lo_ap, in0=src_ap, in1=hi_ap,
                                  op=ALU.subtract))


def mm3(nc, p, out_ap, lh, ll, rh, rl, start, stop):
    p.inc(nc.tensor.matmul(out_ap, lh, rh, start=start, stop=False), "tm")
    p.inc(nc.tensor.matmul(out_ap, lh, rl, start=False, stop=False), "tm")
    p.inc(nc.tensor.matmul(out_ap, ll, rh, start=False, stop=stop), "tm")


def ln_rows(nc, p, ps_s, ps_q, trow, trow2, rrow, rstd_out, mr_out, eps_ap):
    p.v(lambda: nc.vector.tensor_scalar_mul(trow, ps_s, 1.0 / DM))
    p.v(lambda: nc.vector.tensor_scalar_mul(trow2, ps_q, 1.0 / DM))
    p.v(lambda: nc.vector.tensor_tensor(out=rrow, in0=trow, in1=trow, op=ALU.mult))
    p.v(lambda: nc.vector.tensor_tensor(out=trow2, in0=trow2, in1=rrow,
                                  op=ALU.subtract))
    va = p.c["vc"]
    p.wait(nc.scalar, "vc", va)
    sa = p.s(lambda: nc.scalar.activation(out=rrow, in_=trow2, func=AF.Sqrt,
                                    bias=eps_ap))
    p.wait(nc.vector, "sa", sa)
    p.v(lambda: nc.vector.reciprocal(out=rstd_out, in_=rrow))
    p.v(lambda: nc.vector.tensor_tensor(out=mr_out, in0=trow, in1=rstd_out,
                                  op=ALU.mult))


def build_full(debug=False):
    nc = bass.Bass("TRN2", target_bir_lowering=False, dynamic_dma_scratch_size=2048)
    dbg = dict(kind="ExternalOutput") if debug else {}
    xT = nc.dram_tensor("xT", [DM, S], F32, kind="ExternalInput").ap()
    wq = nc.dram_tensor("wq", [DM, HPC * HD], F32, kind="ExternalInput").ap()
    wk = nc.dram_tensor("wk", [DM, HPC * HD], F32, kind="ExternalInput").ap()
    wv = nc.dram_tensor("wv", [DM, HPC * HD], F32, kind="ExternalInput").ap()
    wo = nc.dram_tensor("wo", [HPC * HD, DM], F32, kind="ExternalInput").ap()
    mask01 = nc.dram_tensor("mask01", [128, 128], F32, kind="ExternalInput").ap()
    xTm = nc.dram_tensor("xTm", [DM, TPC], F32, kind="ExternalInput").ap()
    wgT = nc.dram_tensor("wgT", [NEXP, DM, FE], F32, kind="ExternalInput").ap()
    wuT = nc.dram_tensor("wuT", [NEXP, DM, FE], F32, kind="ExternalInput").ap()
    wdT = nc.dram_tensor("wdT", [NEXP, FE, DM], F32, kind="ExternalInput").ap()
    wsgT = nc.dram_tensor("wsgT", [DM, FS], F32, kind="ExternalInput").ap()
    wsuT = nc.dram_tensor("wsuT", [DM, FS], F32, kind="ExternalInput").ap()
    wsdT = nc.dram_tensor("wsdT", [FS, DM], F32, kind="ExternalInput").ap()
    wrT = nc.dram_tensor("wrT", [DM, 8], F32, kind="ExternalInput").ap()
    identf = nc.dram_tensor("identf", [128, 128], F32, kind="ExternalInput").ap()
    y = nc.dram_tensor("y", [TPC, DM], F32, kind="ExternalOutput").ap()
    rs_g = [nc.dram_tensor(f"rs_g{g}", [NCORE, 512, TPC], F32).ap()
            for g in range(4)]
    rso_g = [nc.dram_tensor(f"rso_g{g}", [512, TPC], F32).ap() for g in range(4)]

    ag_in = nc.dram_tensor("ag_in", [2, TPC], F32).ap()
    stats_g = nc.dram_tensor("stats_g", [NCORE * 2, TPC], F32).ap()
    d_xnew = nc.dram_tensor("d_xnew", [128, NDC * TPC], F32, **dbg).ap()
    d_rm = nc.dram_tensor("d_rm", [2, TPC], F32, **dbg).ap()
    d_lnh = nc.dram_tensor("d_lnh", [128, NDC * 512], BF16, **dbg).ap()
    d_lnl = nc.dram_tensor("d_lnl", [128, NDC * 512], BF16, **dbg).ap()
    d_hh = nc.dram_tensor("d_hh", [128, 11 * 512], BF16, **dbg).ap()
    d_hl = nc.dram_tensor("d_hl", [128, 11 * 512], BF16, **dbg).ap()
    d_dwh = nc.dram_tensor("d_dwh", [128, FE], BF16, **dbg).ap()
    d_dwl = nc.dram_tensor("d_dwl", [128, FE], BF16, **dbg).ap()
    d_x2 = nc.dram_tensor("d_x2", [128, NDC * TPC], F32, **dbg).ap()
    d_x3 = nc.dram_tensor("d_x3", [128, NDC * TPC], F32, **dbg).ap()
    qhi_d = nc.dram_tensor("qhi_d", [HPC, HD, S], BF16, **dbg).ap()
    qlo_d = nc.dram_tensor("qlo_d", [HPC, HD, S], BF16, **dbg).ap()
    khi_d = nc.dram_tensor("khi_d", [HPC, HD, S], BF16, **dbg).ap()
    klo_d = nc.dram_tensor("klo_d", [HPC, HD, S], BF16, **dbg).ap()
    vhi_d = nc.dram_tensor("vhi_d", [HPC, 128, S], BF16, **dbg).ap()
    vlo_d = nc.dram_tensor("vlo_d", [HPC, 128, S], BF16, **dbg).ap()

    with ExitStack() as st:
        E = st.enter_context
        p = P(nc, st, ["ld", "lq", "lq0", "lq1", "lkv", "lx0", "lx1", "lw",
                       "lp0", "lp1", "lp2", "lp3", "sd", "so0", "so1", "lg",
                       "tm", "vc", "sa", "ph", "cc"])

        p.vdrain = False
        ones_r = E(nc.sbuf_tensor("ones_r", [1, 128], F32))
        ones_c = E(nc.sbuf_tensor("ones_c", [128, 1], BF16))
        mask_s = E(nc.sbuf_tensor("mask_s", [128, 128], F32))
        eps1 = E(nc.sbuf_tensor("eps1", [1, 1], F32))
        sclc = E(nc.sbuf_tensor("sclc", [128, 1], F32))
        expbc = E(nc.sbuf_tensor("expbc", [128, 1], F32))
        ident_s = E(nc.sbuf_tensor("ident_s", [128, 128], F32))
        ascope = ExitStack()
        o_hi = [ascope.enter_context(nc.sbuf_tensor(f"o_hi{h}", [128, S], BF16))
                for h in range(HPC)]
        o_lo = [ascope.enter_context(nc.sbuf_tensor(f"o_lo{h}", [128, S], BF16))
                for h in range(HPC)]

        p.v(lambda: nc.vector.memset(ones_r[:], 1.0))
        p.v(lambda: nc.vector.memset(ones_c[:], 1.0))
        p.v(lambda: nc.vector.memset(eps1[:], EPS))
        p.v(lambda: nc.vector.memset(sclc[:], SCL))
        p.v(lambda: nc.vector.memset(expbc[:], EXPB))
        init_vc = p.c["vc"]
        p.wait(nc.scalar, "vc", init_vc)
        p.dma(nc.sync, mask_s[:], mask01)
        p.dma(nc.sync, ident_s[:], identf)

        with ExitStack() as ph:
            Eh = ph.enter_context
            xts = [Eh(nc.sbuf_tensor(f"a0x{i}", [128, 512], F32)) for i in range(2)]
            xbf = [Eh(nc.sbuf_tensor(f"a0b{i}", [128, 512], BF16)) for i in range(2)]
            sqb = [Eh(nc.sbuf_tensor(f"a0q{i}", [128, 512], BF16)) for i in range(2)]
            trow = Eh(nc.sbuf_tensor("a0t1", [1, 512], F32))
            trow2 = Eh(nc.sbuf_tensor("a0t2", [1, 512], F32))
            rrow = Eh(nc.sbuf_tensor("a0r", [1, 512], F32))
            mrow = Eh(nc.sbuf_tensor("a0m", [1, 512], F32))
            ps_s = Eh(nc.psum_tensor("a0ps", [1, 512], F32))
            ps_q = Eh(nc.psum_tensor("a0pq", [1, 512], F32))

            p.wait(nc.tensor, "vc", init_vc)
            it_tm = {}
            row_sd = 0
            for tb in range(2):
                for dc in range(NDC):
                    it = tb * NDC + dc
                    i = it % 2
                    if it >= 2:
                        p.wait(nc.sync, "tm", it_tm[it - 2])
                    v = p.dma(nc.sync, xts[i][:],
                              xTm[dc * 128:(dc + 1) * 128, tb * 512:(tb + 1) * 512],
                              f"lx{i}")
                    p.wait(nc.scalar, f"lx{i}", v)
                    if it >= 2:
                        p.wait(nc.scalar, "tm", it_tm[it - 2])
                    p.s(lambda: nc.scalar.activation(out=xbf[i][:], in_=xts[i][:],
                                               func=AF.Copy))
                    sa = p.s(lambda: nc.scalar.activation(out=sqb[i][:], in_=xts[i][:],
                                                    func=AF.Square))
                    p.wait(nc.tensor, "sa", sa)
                    p.inc(nc.tensor.matmul(ps_s[:], ones_c[:], xbf[i][:],
                                           start=(dc == 0), stop=(dc == NDC - 1)), "tm")
                    it_tm[it] = p.inc(nc.tensor.matmul(ps_q[:], ones_c[:], sqb[i][:],
                                                       start=(dc == 0),
                                                       stop=(dc == NDC - 1)), "tm")
                p.wait(nc.vector, "tm", p.c["tm"])
                if tb >= 1:
                    p.wait(nc.vector, "sd", row_sd)
                    p.wait(nc.scalar, "sd", row_sd)
                ln_rows(nc, p, ps_s[:], ps_q[:], trow[:], trow2[:], rrow[:],
                        mrow[:], trow[:], eps1[:])
                p.wait(nc.tensor, "vc", p.c["vc"])
                p.wait(nc.scalar, "vc")
                p.dma(nc.scalar, ag_in[0:1, tb * 512:(tb + 1) * 512], mrow[:], "sd")
                row_sd = p.dma(nc.scalar, ag_in[1:2, tb * 512:(tb + 1) * 512],
                               trow[:], "sd")
            p.wait(nc.gpsimd, "sd", p.c["sd"])
            cc_v = p.inc(nc.gpsimd.collective_compute(
                "AllGather", mybir.AluOpType.bypass,
                replica_groups=[list(range(NCORE))],
                ins=[ag_in[:].opt()], outs=[stats_g[:].opt()]), "cc", 1)
            p.wait(nc.gpsimd, "cc", cc_v)
            barrier(nc, p)

        with ExitStack() as ph:
            Eh = ph.enter_context
            xts = [Eh(nc.sbuf_tensor(f"a1x{i}", [128, 512], F32)) for i in range(2)]
            w_h, w_l = {}, {}
            for nm in ("q", "k", "v"):
                w_h[nm] = Eh(nc.sbuf_tensor(f"a1w{nm}h", [128, NDC * 256], BF16))
                w_l[nm] = Eh(nc.sbuf_tensor(f"a1w{nm}l", [128, NDC * 256], BF16))
            srow = Eh(nc.sbuf_tensor("a1sr", [1, 512], F32))
            mrow = Eh(nc.sbuf_tensor("a1mr", [1, 512], F32))
            r_bb = Eh(nc.sbuf_tensor("a1rb", [128, 512], F32))
            m_bb = Eh(nc.sbuf_tensor("a1mb", [128, 512], F32))
            lnf = [Eh(nc.sbuf_tensor(f"a1ln{i}", [128, 512], F32)) for i in range(2)]
            lnh = [Eh(nc.sbuf_tensor(f"a1lh{i}", [128, 512], BF16)) for i in range(2)]
            lnl = [Eh(nc.sbuf_tensor(f"a1ll{i}", [128, 512], BF16)) for i in range(2)]
            qk_st = [Eh(nc.sbuf_tensor(f"a1s{i}", [128, 512], BF16)) for i in range(8)]
            v_st = [Eh(nc.sbuf_tensor(f"a1v{i}", [128, 1024], BF16)) for i in range(2)]
            ps_qk = [Eh(nc.psum_tensor(f"a1p{i}", [128, 512], F32)) for i in range(4)]
            ps_v = [Eh(nc.psum_tensor(f"a1pv{i}", [128, 512], F32)) for i in range(2)]
            ps_bc = Eh(nc.psum_tensor("a1pb", [128, 512], F32))

            with nc.sbuf_tensor("a1wf", [128, NDC * 256], F32) as wf:
                for nm, w_ap in (("q", wq), ("k", wk), ("v", wv)):
                    v = p.dma(nc.sync, wf[:].rearrange("p (c m) -> p c m", m=256),
                              w_ap.rearrange("(c p) m -> p c m", p=128))
                    p.wait(nc.scalar, "ld", v)
                    split_hi_lo(nc, p, wf[:], w_h[nm][:], w_l[nm][:])
                    p.wait(nc.sync, "vc", p.c["vc"])

            ln_done = [0] * NTB
            dc_tm = {}
            dc_vc = {}
            qkv_sd = 0
            bc_tm = 0
            for tb in range(NTB):
                p.wait(nc.sync, "tm", bc_tm)
                gb, hb = tb // 2, (tb % 2) * 512
                p.dma(nc.sync, srow[:], stats_g[2 * gb:2 * gb + 1, hb:hb + 512], "lq")
                v = p.dma(nc.sync, mrow[:],
                          stats_g[2 * gb + 1:2 * gb + 2, hb:hb + 512], "lq")
                p.wait(nc.tensor, "lq", v)
                if tb >= 1:
                    p.wait(nc.tensor, "vc", ln_done[tb - 1])
                t1 = p.inc(nc.tensor.matmul(ps_bc[:], ones_r[:], srow[:],
                                            start=True, stop=True), "tm")
                p.wait(nc.vector, "tm", t1)
                p.v(lambda: nc.vector.tensor_copy(out=r_bb[:], in_=ps_bc[:]))
                p.wait(nc.tensor, "vc", p.c["vc"])
                t2 = p.inc(nc.tensor.matmul(ps_bc[:], ones_r[:], mrow[:],
                                            start=True, stop=True), "tm")
                bc_tm = t2
                p.wait(nc.vector, "tm", t2)
                p.v(lambda: nc.vector.tensor_copy(out=m_bb[:], in_=ps_bc[:]))

                for dc in range(NDC):
                    li = dc % 2
                    it = tb * NDC + dc
                    if it >= 2:
                        p.wait(nc.sync, "vc", dc_vc[it - 2])
                    ldv = p.dma(nc.sync, xts[li][:],
                                xT[dc * 128:(dc + 1) * 128, tb * 512:(tb + 1) * 512],
                                f"lx{li}")
                    xin = xts[li][:]
                    p.wait(nc.vector, f"lx{li}", ldv)
                    if it >= 2:
                        p.wait(nc.vector, "tm", dc_tm[it - 2])
                        p.wait(nc.scalar, "tm", dc_tm[it - 2])
                    p.v(lambda: nc.vector.tensor_tensor(out=lnf[li][:], in0=xin, in1=r_bb[:],
                                                  op=ALU.mult))
                    p.v(lambda: nc.vector.tensor_tensor(out=lnf[li][:], in0=lnf[li][:],
                                                  in1=m_bb[:], op=ALU.subtract))
                    dc_vc[it] = p.c["vc"]
                    split_hi_lo(nc, p, lnf[li][:], lnh[li][:], lnl[li][:])
                    p.wait(nc.tensor, "vc")
                    p.wait(nc.tensor, "sa")
                    first, last = dc == 0, dc == NDC - 1
                    for ti, nm in enumerate(("q", "k")):
                        for h in range(HPC):
                            lsl = slice(dc * 256 + h * 128, dc * 256 + (h + 1) * 128)
                            mm3(nc, p, ps_qk[ti * 2 + h][:], w_h[nm][:, lsl],
                                w_l[nm][:, lsl], lnh[li][:], lnl[li][:], first, last)
                    wsl = slice(dc * 256, (dc + 1) * 256)
                    for sub in range(4):
                        ssl = slice(sub * 128, (sub + 1) * 128)
                        pv = ps_v[sub // 2][:, (sub % 2) * 256:(sub % 2 + 1) * 256]
                        st0 = first and (sub % 2 == 0)
                        sp0 = last and (sub % 2 == 1)
                        p.inc(nc.tensor.matmul(pv, lnh[li][:, ssl], w_h["v"][:, wsl],
                                               start=st0, stop=False), "tm")
                        p.inc(nc.tensor.matmul(pv, lnl[li][:, ssl], w_h["v"][:, wsl],
                                               start=False, stop=False), "tm")
                        p.inc(nc.tensor.matmul(pv, lnh[li][:, ssl], w_l["v"][:, wsl],
                                               start=False, stop=sp0), "tm")
                    dc_tm[it] = p.c["tm"]
                ln_done[tb] = p.c["vc"]
                tm_qkv = p.c["tm"]

                p.wait(nc.scalar, "tm", tm_qkv)
                p.wait(nc.vector, "tm", tm_qkv)
                if tb >= 1:
                    p.wait(nc.scalar, "sd", qkv_sd)
                    p.wait(nc.vector, "sd", qkv_sd)
                for i in range(4):
                    split_hi_lo(nc, p, ps_qk[i][:], qk_st[2 * i][:], qk_st[2 * i + 1][:])
                for i in range(2):
                    split_hi_lo(nc, p, ps_v[i][:], v_st[i][:, 0:512],
                                v_st[i][:, 512:1024])
                p.wait(nc.scalar, "vc")
                tsl = slice(tb * 512, (tb + 1) * 512)
                dmap = [(qhi_d, 0, 0), (qlo_d, 0, 1), (qhi_d, 1, 2), (qlo_d, 1, 3),
                        (khi_d, 0, 4), (klo_d, 0, 5), (khi_d, 1, 6), (klo_d, 1, 7)]
                for dten, h, sti in dmap:
                    p.dma(nc.scalar, dten[h, :, tsl], qk_st[sti][:], "sd")
                for i in range(2):
                    for half, dten in ((0, vhi_d), (1, vlo_d)):
                        for h in range(HPC):
                            src = v_st[i][:].rearrange("p (a s h d) -> p a s h d",
                                                       a=2, s=2, h=2)[:, half, :, h, :]
                            k0 = tb * 4 + 2 * i
                            dst = dten[h, :, k0 * 128:(k0 + 2) * 128].rearrange(
                                "p (s d) -> p s d", d=128)
                            p.dma(nc.scalar, dst, src, "sd")
                qkv_sd = p.c["sd"]
            p.wait(nc.gpsimd, "sd", p.c["sd"])
            barrier(nc, p)

        with ExitStack() as ph:
            Eh = ph.enter_context
            k_hi = Eh(nc.sbuf_tensor("a2kh", [128, S], BF16))
            k_lo = Eh(nc.sbuf_tensor("a2kl", [128, S], BF16))
            vv_hi = Eh(nc.sbuf_tensor("a2vh", [128, S], BF16))
            vv_lo = Eh(nc.sbuf_tensor("a2vl", [128, S], BF16))
            q_hi = [Eh(nc.sbuf_tensor(f"a2qh{i}", [128, 1024], BF16)) for i in range(2)]
            q_lo = [Eh(nc.sbuf_tensor(f"a2ql{i}", [128, 1024], BF16)) for i in range(2)]
            pf = [Eh(nc.sbuf_tensor(f"a2pf{i}", [128, 1024], F32)) for i in range(2)]
            p_hi = [Eh(nc.sbuf_tensor(f"a2ph{i}", [128, 1024], BF16)) for i in range(2)]
            p_lo = [Eh(nc.sbuf_tensor(f"a2pl{i}", [128, 1024], BF16)) for i in range(2)]
            zrow = Eh(nc.sbuf_tensor("a2z", [1, 1024], F32))
            rzrow = Eh(nc.sbuf_tensor("a2rz", [1, 1024], F32))
            rz_bb = Eh(nc.sbuf_tensor("a2rzb", [128, 1024], F32))
            colv = Eh(nc.sbuf_tensor("a2cv", [128, 1], F32))
            otf = Eh(nc.sbuf_tensor("a2ot", [128, 1024], F32))
            ps_st = [Eh(nc.psum_tensor(f"a2ps{i}", [128, 1024], F32)) for i in range(2)]
            ps_ot = Eh(nc.psum_tensor("a2po", [128, 1024], F32))
            ps_z = Eh(nc.psum_tensor("a2pz", [1, 1024], F32))

            gq_tm = {}
            gq = 0
            kv_tm = 0
            for h in range(HPC):
                p.wait(nc.sync, "tm", kv_tm)
                p.dma(nc.sync, k_hi[:], khi_d[h], "lkv")
                p.dma(nc.sync, k_lo[:], klo_d[h], "lkv")
                p.dma(nc.sync, vv_hi[:], vhi_d[h], "lkv")
                lk = p.dma(nc.sync, vv_lo[:], vlo_d[h], "lkv")
                p.wait(nc.tensor, "lkv", lk)

                for kt in range(NT):
                    p.inc(nc.tensor.matmul(ps_st[0][:, 0:1],
                                           vv_hi[:, kt * 128:(kt + 1) * 128], ones_c[:],
                                           start=(kt == 0), stop=False), "tm")
                    p.inc(nc.tensor.matmul(ps_st[0][:, 0:1],
                                           vv_lo[:, kt * 128:(kt + 1) * 128], ones_c[:],
                                           start=False, stop=(kt == NT - 1)), "tm")
                p.wait(nc.vector, "tm", p.c["tm"])
                p.v(lambda: nc.vector.tensor_copy(out=colv[:], in_=ps_st[0][:, 0:1]))
                p.wait(nc.tensor, "vc", p.c["vc"])

                for qg in range(8):
                    qb = gq % 2
                    qsl = slice(qg * 1024, (qg + 1) * 1024)
                    if gq >= 2:
                        p.wait(nc.sync, "tm", gq_tm[gq - 2])
                    p.dma(nc.sync, q_hi[qb][:], qhi_d[h, :, qsl], f"lq{qb}")
                    lq2 = p.dma(nc.sync, q_lo[qb][:], qlo_d[h, :, qsl], f"lq{qb}")
                    p.wait(nc.tensor, f"lq{qb}", lq2)

                    kts = list(range(8 * qg, NT))
                    nk = len(kts)
                    exp_done, split_done, vz_done = {}, {}, {}
                    sc_tm = {}

                    def emit_scores(j):
                        ktj = kts[j]
                        ncj = 128 * (min(ktj, 8 * qg + 7) - 8 * qg + 1)
                        sbj = j % 2
                        kslj = slice(ktj * 128, (ktj + 1) * 128)
                        if j >= 2:
                            p.wait(nc.tensor, "sa", exp_done[j - 2])
                        for c0 in range(0, ncj, 512):
                            c1 = min(c0 + 512, ncj)
                            mm3(nc, p, ps_st[sbj][:, c0:c1], k_hi[:, kslj],
                                k_lo[:, kslj], q_hi[qb][:, c0:c1],
                                q_lo[qb][:, c0:c1], True, True)
                            sc_tm[(j, c0)] = p.c["tm"]

                    emit_scores(0)
                    for i, kt in enumerate(kts):
                        ncols = 128 * (min(kt, 8 * qg + 7) - 8 * qg + 1)
                        sb = i % 2
                        ksl = slice(kt * 128, (kt + 1) * 128)
                        if i + 1 < nk:
                            emit_scores(i + 1)
                        fin = kt == NT - 1
                        first_chunk = True
                        for e0 in range(0, ncols, 512):
                            e1 = min(e0 + 512, ncols)
                            p.wait(nc.scalar, "tm", sc_tm[(i, e0)])
                            if first_chunk and i >= 2:
                                p.wait(nc.scalar, "vc", split_done[i - 2])
                                p.wait(nc.scalar, "tm", vz_done[i - 2])
                            sa = p.s(lambda: nc.scalar.activation(
                                out=pf[sb][:, e0:e1], in_=ps_st[sb][:, e0:e1],
                                func=AF.Exp, bias=expbc[:], scale=sclc[:]))
                            p.wait(nc.vector, "sa", sa)
                            if first_chunk and i >= 2:
                                p.wait(nc.vector, "tm", vz_done[i - 2])
                            first_chunk = False
                            diag_here = kt <= 8 * qg + 7 and e1 == ncols
                            if diag_here:
                                p.v(lambda: nc.vector.tensor_tensor(
                                    out=pf[sb][:, ncols - 128:ncols],
                                    in0=pf[sb][:, ncols - 128:ncols],
                                    in1=mask_s[:], op=ALU.mult))
                            p.v(lambda: nc.vector.tensor_copy(
                                out=p_hi[sb][:, e0:e1], in_=pf[sb][:, e0:e1]))
                            p.v(lambda: nc.vector.tensor_tensor(
                                out=p_lo[sb][:, e0:e1], in0=pf[sb][:, e0:e1],
                                in1=p_hi[sb][:, e0:e1], op=ALU.subtract))
                            ck_vc = p.c["vc"]
                            p.wait(nc.tensor, "vc", ck_vc)
                            cuts = {e0, e1}
                            if diag_here and ncols - 128 > e0:
                                cuts.add(ncols - 128)
                            cuts = sorted(cuts)
                            for c0, c1 in zip(cuts[:-1], cuts[1:]):
                                stt = (c0 == 0 and kt == 8 * qg) or \
                                      (c0 == 512 and kt == 8 * qg + 4)
                                seg_fin = fin and c1 == e1
                                p.inc(nc.tensor.matmul(ps_ot[:, c0:c1], vv_hi[:, ksl],
                                                       p_hi[sb][:, c0:c1],
                                                       start=stt, stop=False), "tm")
                                p.inc(nc.tensor.matmul(ps_ot[:, c0:c1], vv_hi[:, ksl],
                                                       p_lo[sb][:, c0:c1],
                                                       start=False, stop=False), "tm")
                                p.inc(nc.tensor.matmul(ps_ot[:, c0:c1], vv_lo[:, ksl],
                                                       p_hi[sb][:, c0:c1],
                                                       start=False, stop=seg_fin),
                                      "tm")
                                p.inc(nc.tensor.matmul(ps_z[:, c0:c1], ones_c[:],
                                                       p_hi[sb][:, c0:c1],
                                                       start=stt, stop=False), "tm")
                                p.inc(nc.tensor.matmul(ps_z[:, c0:c1], ones_c[:],
                                                       p_lo[sb][:, c0:c1],
                                                       start=False, stop=seg_fin),
                                      "tm")
                        exp_done[i] = p.c["sa"]
                        split_done[i] = p.c["vc"]
                        vz_done[i] = p.c["tm"]
                    gq_tm[gq] = p.c["tm"]
                    av_tm = p.c["tm"]

                    p.wait(nc.vector, "tm", av_tm)
                    p.v(lambda: nc.vector.tensor_copy(out=zrow[:], in_=ps_z[:]))
                    if qg == 7:
                        p.v(lambda: nc.vector.memset(zrow[:, 1023:1024], 1.0))
                    p.v(lambda: nc.vector.reciprocal(out=rzrow[:], in_=zrow[:]))
                    p.wait(nc.tensor, "vc", p.c["vc"])
                    p.inc(nc.tensor.matmul(ps_st[0][:, 0:512], ones_r[:],
                                           rzrow[:, 0:512], start=True, stop=True),
                          "tm")
                    t2 = p.inc(nc.tensor.matmul(ps_st[0][:, 512:1024], ones_r[:],
                                                rzrow[:, 512:1024], start=True,
                                                stop=True), "tm")
                    p.wait(nc.vector, "tm", t2)
                    p.v(lambda: nc.vector.tensor_copy(out=rz_bb[:], in_=ps_st[0][:]))
                    p.v(lambda: nc.vector.tensor_tensor(out=otf[:], in0=ps_ot[:],
                                                  in1=rz_bb[:], op=ALU.mult))
                    if qg == 7:
                        p.v(lambda: nc.vector.tensor_scalar_mul(otf[:, 1023:1024], colv[:],
                                                          1.0 / S))
                    split_hi_lo(nc, p, otf[:], o_hi[h][:, qsl], o_lo[h][:, qsl])
                    p.wait(nc.tensor, "vc", p.c["vc"])
                    gq += 1
                kv_tm = p.c["tm"]
            barrier(nc, p)
        p.vdrain = True

        with ExitStack() as ph:
            Eh = ph.enter_context
            wof = Eh(nc.sbuf_tensor("a3wf", [128, HPC * DM], F32))
            wo_h = Eh(nc.sbuf_tensor("a3wh", [128, HPC * DM], BF16))
            wo_l = Eh(nc.sbuf_tensor("a3wl", [128, HPC * DM], BF16))
            ost = [Eh(nc.sbuf_tensor(f"a3o{i}", [128, 1024], F32)) for i in range(2)]
            ps_o = [Eh(nc.psum_tensor(f"a3p{i}", [128, 1024], F32)) for i in range(2)]

            v = p.dma(nc.sync, wof[:].rearrange("p (h n) -> p h n", h=HPC),
                      wo.rearrange("(h p) n -> p h n", p=128))
            p.wait(nc.scalar, "ld", v)
            split_hi_lo(nc, p, wof[:], wo_h[:], wo_l[:])
            p.wait(nc.tensor, "vc", p.c["vc"])
            p.wait(nc.tensor, "sa", p.c["sa"])

            it_sa = {}
            it_sd = {}
            cc_at = {}
            it = 0
            for g in range(4):
                for d2 in range(g * 4, g * 4 + 4):
                    for tc in range(NCORE):
                        tsl = slice(tc * 1024, (tc + 1) * 1024)
                        i = it % 2
                        if it >= 2:
                            p.wait(nc.tensor, "sa", it_sa[it - 2])
                        for c0 in (0, 512):
                            csl = slice(c0, c0 + 512)
                            for hh in range(HPC):
                                lsl = slice(hh * DM + d2 * 128,
                                            hh * DM + (d2 + 1) * 128)
                                mm3(nc, p, ps_o[i][:, csl], wo_h[:, lsl],
                                    wo_l[:, lsl], o_hi[hh][:, tsl][:, csl],
                                    o_lo[hh][:, tsl][:, csl],
                                    start=(hh == 0), stop=(hh == HPC - 1))
                        tmv = p.c["tm"]
                        p.wait(nc.scalar, "tm", tmv)
                        if it >= 2:
                            p.wait(nc.scalar, f"so{i}", it_sd[it - 2])
                        it_sa[it] = p.s(lambda: nc.scalar.activation(
                            out=ost[i][:], in_=ps_o[i][:], func=AF.Copy))
                        it_sd[it] = p.dma(
                            nc.scalar,
                            rs_g[g][tc, (d2 % 4) * 128:(d2 % 4 + 1) * 128, :],
                            ost[i][:], f"so{i}")
                        it += 1
                p.wait(nc.gpsimd, "so0", p.c["so0"])
                p.wait(nc.gpsimd, "so1", p.c["so1"])
                cc_at[g] = p.inc(nc.gpsimd.collective_compute(
                    "ReduceScatter", mybir.AluOpType.add,
                    replica_groups=[list(range(NCORE))],
                    ins=[rs_g[g][:].opt()], outs=[rso_g[g][:].opt()]), "cc", 1)
        ascope.close()
        barrier(nc, p)
        xnewT = E(nc.sbuf_tensor("xnewT", [128, NDC * TPC], F32))
        rstd = E(nc.sbuf_tensor("b1rs", [1, TPC], F32))
        mr = E(nc.sbuf_tensor("b1mrr", [1, TPC], F32))

        with ExitStack() as ph:
            Eh = ph.enter_context
            pin = [Eh(nc.sbuf_tensor(f"b0p{i}", [128, TPC], F32)) for i in range(4)]
            done_vc = {}
            for dc in range(NDC):
                dsl = slice((dc % 4) * 128, (dc % 4 + 1) * 128)
                xsl = slice(dc * TPC, (dc + 1) * TPC)
                i0, i1 = (dc % 2) * 2, (dc % 2) * 2 + 1
                if dc >= 2:
                    p.wait(nc.sync, "vc", done_vc[dc - 2])
                p.dma(nc.sync, pin[i0][:], rso_g[dc // 4][dsl, :], f"lp{i0}")
                v1 = p.dma(nc.sync, pin[i1][:], xTm[dc * 128:(dc + 1) * 128, :],
                           f"lp{i1}")
                p.wait(nc.vector, f"lp{i0}", p.c[f"lp{i0}"])
                p.wait(nc.vector, f"lp{i1}", v1)
                p.v(lambda: nc.vector.tensor_tensor(out=xnewT[:, xsl], in0=pin[i0][:],
                                                    in1=pin[i1][:], op=ALU.add))
                done_vc[dc] = p.c["vc"]
            barrier(nc, p)
        with ExitStack() as ph:
            Eh = ph.enter_context
            xbf = [Eh(nc.sbuf_tensor(f"b1xb{i}", [128, TPC], BF16)) for i in range(2)]
            sqb = [Eh(nc.sbuf_tensor(f"b1sq{i}", [128, TPC], BF16)) for i in range(2)]
            trow = Eh(nc.sbuf_tensor("b1t1", [1, TPC], F32))
            trow2 = Eh(nc.sbuf_tensor("b1t2", [1, TPC], F32))
            rrow = Eh(nc.sbuf_tensor("b1r", [1, TPC], F32))
            ps_s = Eh(nc.psum_tensor("b1ps", [1, TPC], F32))
            ps_q = Eh(nc.psum_tensor("b1pq", [1, TPC], F32))
            ones_cb = Eh(nc.sbuf_tensor("b1oc", [128, 1], BF16))
            p.v(lambda: nc.vector.memset(ones_cb[:], 1.0))
            oc_vc = p.c["vc"]

            it_tm = {}
            p.wait(nc.tensor, "vc", oc_vc)
            for dc in range(NDC):
                i = dc % 2
                xin = xnewT[:, dc * TPC:(dc + 1) * TPC]
                if dc >= 2:
                    p.wait(nc.scalar, "tm", it_tm[dc - 2])
                p.s(lambda: nc.scalar.activation(out=xbf[i][:], in_=xin, func=AF.Copy))
                sa = p.s(lambda: nc.scalar.activation(out=sqb[i][:], in_=xin,
                                                func=AF.Square))
                p.wait(nc.tensor, "sa", sa)
                for c0 in (0, 512):
                    csl = slice(c0, c0 + 512)
                    p.inc(nc.tensor.matmul(ps_s[:, csl], ones_cb[:], xbf[i][:, csl],
                                           start=(dc == 0), stop=(dc == NDC - 1)), "tm")
                    p.inc(nc.tensor.matmul(ps_q[:, csl], ones_cb[:], sqb[i][:, csl],
                                           start=(dc == 0), stop=(dc == NDC - 1)), "tm")
                it_tm[dc] = p.c["tm"]
            p.wait(nc.vector, "tm", p.c["tm"])
            ln_rows(nc, p, ps_s[:], ps_q[:], trow[:], trow2[:], rrow[:],
                    rstd[:], mr[:], eps1[:])
            if debug:
                p.wait(nc.scalar, "vc")
                p.dma(nc.scalar, d_rm[0:1, :], rstd[:], "sd")
                p.dma(nc.scalar, d_rm[1:2, :], mr[:], "sd")
                p.dma(nc.scalar, d_xnew[:], xnewT[:], "sd")
            barrier(nc, p)

        TP2 = 512
        p.vdrain = False
        with ExitStack() as ph:
            Eh = ph.enter_context
            ln_h2 = Eh(nc.sbuf_tensor("b4lnh", [128, NDC * TP2], BF16))
            ln_l2 = Eh(nc.sbuf_tensor("b4lnl", [128, NDC * TP2], BF16))
            lnf2 = [Eh(nc.sbuf_tensor(f"b4lf{i}", [128, TP2], F32)) for i in range(2)]
            r_bb = Eh(nc.sbuf_tensor("b4rb", [128, TP2], F32))
            m_bb = Eh(nc.sbuf_tensor("b4mb", [128, TP2], F32))
            ust = [Eh(nc.sbuf_tensor(f"b4us{i}", [128, DM], F32)) for i in range(2)]
            up_h = [[Eh(nc.sbuf_tensor(f"b4u{k}h{b}", [128, DM], BF16))
                     for b in range(2)] for k in range(2)]
            up_l = [[Eh(nc.sbuf_tensor(f"b4u{k}l{b}", [128, DM], BF16))
                     for b in range(2)] for k in range(2)]
            dn_h = [Eh(nc.sbuf_tensor(f"b4dh{i}", [128, FE], BF16)) for i in range(2)]
            dn_l = [Eh(nc.sbuf_tensor(f"b4dl{i}", [128, FE], BF16)) for i in range(2)]
            h_hi = Eh(nc.sbuf_tensor("b4hh", [128, 11 * TP2], BF16))
            h_lo = Eh(nc.sbuf_tensor("b4hl", [128, 11 * TP2], BF16))
            hf = Eh(nc.sbuf_tensor("b4hf", [128, TP2], F32))
            w_bb = Eh(nc.sbuf_tensor("b4wb", [128, TP2], F32))
            wrow = Eh(nc.sbuf_tensor("b4wro", [1, NEXP * TP2], F32))
            wr_h = Eh(nc.sbuf_tensor("b4wrh", [128, NDC * 8], BF16))
            wr_l = Eh(nc.sbuf_tensor("b4wrl", [128, NDC * 8], BF16))
            lg8 = Eh(nc.sbuf_tensor("b4lg8", [8, TP2], F32))
            ltok = Eh(nc.sbuf_tensor("b4lt", [128, 4 * NEXP], F32))
            m1 = Eh(nc.sbuf_tensor("b4m1", [128, 4], F32))
            zr = Eh(nc.sbuf_tensor("b4zr", [128, 4], F32))
            prob = Eh(nc.sbuf_tensor("b4p", [128, 4 * NEXP], F32))
            tmp = Eh(nc.sbuf_tensor("b4tm2", [128, 4 * NEXP], F32))
            wtok = Eh(nc.sbuf_tensor("b4wt", [128, 4 * NEXP], F32))
            ps_g = [Eh(nc.psum_tensor(f"b4pg{i}", [128, TP2], F32)) for i in range(2)]
            ps_u = [Eh(nc.psum_tensor(f"b4pu{i}", [128, TP2], F32)) for i in range(2)]
            ps_o = [Eh(nc.psum_tensor(f"b4po{i}", [128, TP2], F32)) for i in range(2)]
            ps_wb = Eh(nc.psum_tensor("b4pw", [128, TP2], F32))
            ps_r8 = Eh(nc.psum_tensor("b4pr", [8, TP2], F32))

            v = p.dma(nc.sync, ust[0][:, 0:NDC * 8].rearrange("p (c e) -> p c e", e=8),
                      wrT.rearrange("(c p) e -> p c e", p=128), "lw")
            p.wait(nc.scalar, "lw", v)
            split_w(nc, p, ust[0][:, 0:NDC * 8], wr_h[:], wr_l[:])

            stq = [0]
            stage_free = [p.c["vc"], p.c["vc"]]
            up_used_tm = [0, 0]
            up_split_vc = [0, 0]
            up_split_sa = [0, 0]
            silu_sa = [0, 0]
            psu_read_vc = [0, 0]
            dn_used_tm = [0, 0]
            dn_split_vc = [0, 0]
            dn_split_sa = [0, 0]
            xadd_vc = [0, 0]
            wbb_read_vc = [0]
            pswb_read = [0]
            psr_free = [0]
            h_done_vc = [0]
            h_done_sa = [0]
            upi = [0]
            dni = [0]
            item_ld = {}
            dbg_h = [False]
            dbg_d = [False]
            dbg_x2 = [False]

            def up_srcs(task, fc):
                kind, arg = task
                if kind == "s":
                    gfc = arg + fc
                    bases = (wsgT, wsuT)
                    return [b.rearrange("(c p) f -> p c f", p=128)[
                        :, :, gfc * 128:(gfc + 1) * 128] for b in bases]
                return [b[arg].rearrange("(c p) f -> p c f", p=128)[
                    :, :, fc * 128:(fc + 1) * 128] for b in (wgT, wuT)]

            def emit_loads(key, item):
                if item[0] == "up":
                    _, task, fc, b, we = item
                    lands, stages = [], []
                    for src in up_srcs(task, fc):
                        s = stq[0] % 2
                        stq[0] += 1
                        p.wait(nc.sync, "vc", stage_free[s])
                        lands.append(p.dma(
                            nc.sync,
                            ust[s][:].rearrange("p (c f) -> p c f", f=128),
                            src, "lw"))
                        stages.append(s)
                    item_ld[key] = (stages, lands)
                elif item[0] == "down":
                    _, task, dc, b, th = item
                    kind, arg = task
                    s = stq[0] % 2
                    stq[0] += 1
                    dsl = slice(dc * 128, (dc + 1) * 128)
                    if kind == "s":
                        src = wsdT.rearrange("(c p) d -> p c d", p=128)[
                            :, arg:arg + 11, dsl]
                    else:
                        src = wdT[arg].rearrange("(c p) d -> p c d", p=128)[:, :, dsl]
                    p.wait(nc.sync, "vc", stage_free[s])
                    land = p.dma(nc.sync,
                                 ust[s][:, 0:FE].rearrange("p (c d) -> p c d", d=128),
                                 src, "lg")
                    item_ld[key] = ([s], [land])

            def emit_splits(key, item):
                if item[0] == "up":
                    _, task, fc, b, we = item
                    stages, lands = item_ld[key]
                    p.wait(nc.scalar, "tm", up_used_tm[b])
                    for j in range(2):
                        s = stages[j]
                        p.wait(nc.scalar, "lw", lands[j])
                        split_w(nc, p, ust[s][:], up_h[j][b][:], up_l[j][b][:])
                        stage_free[s] = p.c["vc"]
                    up_split_vc[b] = p.c["vc"]
                    up_split_sa[b] = p.c["sa"]
                elif item[0] == "down":
                    _, task, dc, b, th = item
                    (s,), (land,) = item_ld[key]
                    p.wait(nc.scalar, "lg", land)
                    p.wait(nc.scalar, "tm", dn_used_tm[b])
                    split_w(nc, p, ust[s][:, 0:FE], dn_h[b][:], dn_l[b][:])
                    stage_free[s] = p.c["vc"]
                    dn_split_vc[b] = p.c["vc"]
                    dn_split_sa[b] = p.c["sa"]

            def emit_compute(item):
                if item[0] == "up":
                    _, task, fc, b, we = item
                    p.wait(nc.tensor, "vc", up_split_vc[b])
                    p.wait(nc.tensor, "sa", up_split_sa[b])
                    p.wait(nc.tensor, "sa", silu_sa[b])
                    p.wait(nc.tensor, "vc", psu_read_vc[b])
                    for dc in range(NDC):
                        wsl = slice(dc * 128, (dc + 1) * 128)
                        lsl = slice(dc * TP2, (dc + 1) * TP2)
                        first, last = dc == 0, dc == NDC - 1
                        mm3(nc, p, ps_g[b][:], up_h[0][b][:, wsl], up_l[0][b][:, wsl],
                            ln_h2[:, lsl], ln_l2[:, lsl], first, last)
                        mm3(nc, p, ps_u[b][:], up_h[1][b][:, wsl], up_l[1][b][:, wsl],
                            ln_h2[:, lsl], ln_l2[:, lsl], first, last)
                    up_used_tm[b] = p.c["tm"]
                    utm = p.c["tm"]
                    p.wait(nc.scalar, "tm", utm)
                    p.wait(nc.scalar, "vc", h_done_vc[0])
                    sa = p.s(lambda: nc.scalar.activation(out=hf[:], in_=ps_g[b][:],
                                                    func=AF.Silu))
                    silu_sa[b] = sa
                    p.wait(nc.vector, "sa", sa)
                    p.wait(nc.vector, "tm", utm)
                    p.v(lambda: nc.vector.tensor_tensor(out=hf[:], in0=hf[:],
                                                  in1=ps_u[b][:], op=ALU.mult))
                    if we is not None:
                        p.v(lambda: nc.vector.tensor_tensor(out=hf[:], in0=hf[:],
                                                      in1=w_bb[:], op=ALU.mult))
                        wbb_read_vc[0] = p.c["vc"]
                    psu_read_vc[b] = p.c["vc"]
                    hsl = slice(fc * TP2, (fc + 1) * TP2)
                    split_hi_lo(nc, p, hf[:], h_hi[:, hsl], h_lo[:, hsl])
                    h_done_vc[0] = p.c["vc"]
                    h_done_sa[0] = p.c["sa"]
                    if debug and task == ("s", 0) and fc == 10 and not dbg_h[0]:
                        dbg_h[0] = True
                        p.wait(nc.scalar, "vc")
                        p.dma(nc.scalar, d_hh[:], h_hi[:], "sd")
                        p.dma(nc.scalar, d_hl[:], h_lo[:], "sd")
                        p.wait(nc.scalar, "sd", p.c["sd"])
                elif item[0] == "down":
                    _, task, dc, b, th = item
                    p.wait(nc.tensor, "vc", dn_split_vc[b])
                    p.wait(nc.tensor, "sa", dn_split_sa[b])
                    p.wait(nc.tensor, "vc", xadd_vc[b])
                    if dc == 0:
                        p.wait(nc.tensor, "vc", h_done_vc[0])
                        p.wait(nc.tensor, "sa", h_done_sa[0])
                    for fc in range(11):
                        fsl = slice(fc * TP2, (fc + 1) * TP2)
                        mm3(nc, p, ps_o[b][:], dn_h[b][:, fc * 128:(fc + 1) * 128],
                            dn_l[b][:, fc * 128:(fc + 1) * 128],
                            h_hi[:, fsl], h_lo[:, fsl], fc == 0, fc == 10)
                    dn_used_tm[b] = p.c["tm"]
                    dtm = p.c["tm"]
                    p.wait(nc.vector, "tm", dtm)
                    xsl = slice(dc * TPC + th * TP2, dc * TPC + th * TP2 + TP2)
                    p.v(lambda: nc.vector.tensor_tensor(out=xnewT[:, xsl],
                                                  in0=xnewT[:, xsl],
                                                  in1=ps_o[b][:], op=ALU.add))
                    xadd_vc[b] = p.c["vc"]
                    if debug and task == ("s", 0) and dc == 1 and not dbg_d[0]:
                        dbg_d[0] = True
                        p.wait(nc.scalar, "sa", dn_split_sa[b])
                        p.wait(nc.scalar, "vc", dn_split_vc[b])
                        p.dma(nc.scalar, d_dwh[:], dn_h[b][:], "sd")
                        p.dma(nc.scalar, d_dwl[:], dn_l[b][:], "sd")
                        p.wait(nc.scalar, "sd", p.c["sd"])
                    if debug and task == ("s", 0) and dc == NDC - 1 and not dbg_x2[0]:
                        dbg_x2[0] = True
                        p.wait(nc.scalar, "vc")
                        p.dma(nc.scalar, d_x2[:], xnewT[:], "sd")
                        p.wait(nc.scalar, "sd", p.c["sd"])
                else:
                    _, e = item[0], item[1]
                    p.wait(nc.tensor, "vc", wbb_read_vc[0])
                    p.wait(nc.tensor, "vc", pswb_read[0])
                    t = p.inc(nc.tensor.matmul(ps_wb[:], ones_r[:],
                                               wrow[0:1, e * TP2:(e + 1) * TP2],
                                               start=True, stop=True), "tm")
                    p.wait(nc.vector, "tm", t)
                    p.v(lambda: nc.vector.tensor_copy(out=w_bb[:], in_=ps_wb[:]))
                    pswb_read[0] = p.c["vc"]

            def fill_ln_router(th):
                gate_tm = p.c["tm"]
                p.wait(nc.vector, "tm", gate_tm)
                p.wait(nc.scalar, "tm", gate_tm)
                p.wait(nc.tensor, "vc", pswb_read[0])
                tsl = slice(th * TP2, (th + 1) * TP2)
                t1 = p.inc(nc.tensor.matmul(ps_wb[:], ones_r[:], rstd[0:1, tsl],
                                            start=True, stop=True), "tm")
                p.wait(nc.vector, "tm", t1)
                p.v(lambda: nc.vector.tensor_copy(out=r_bb[:], in_=ps_wb[:]))
                p.wait(nc.tensor, "vc", p.c["vc"])
                t2 = p.inc(nc.tensor.matmul(ps_wb[:], ones_r[:], mr[0:1, tsl],
                                            start=True, stop=True), "tm")
                p.wait(nc.vector, "tm", t2)
                p.v(lambda: nc.vector.tensor_copy(out=m_bb[:], in_=ps_wb[:]))
                pswb_read[0] = p.c["vc"]
                p.wait(nc.tensor, "vc", psr_free[0])
                lnf_sa = {}
                for dc in range(NDC):
                    i = dc % 2
                    xin = xnewT[:, dc * TPC + th * TP2: dc * TPC + th * TP2 + TP2]
                    lsl = slice(dc * TP2, (dc + 1) * TP2)
                    if dc >= 2:
                        p.wait(nc.vector, "sa", lnf_sa[dc - 2])
                    p.v(lambda: nc.vector.tensor_tensor(out=lnf2[i][:], in0=xin,
                                                  in1=r_bb[:], op=ALU.mult))
                    p.v(lambda: nc.vector.tensor_tensor(out=lnf2[i][:], in0=lnf2[i][:],
                                                  in1=m_bb[:], op=ALU.subtract))
                    split_hi_lo(nc, p, lnf2[i][:], ln_h2[:, lsl], ln_l2[:, lsl])
                    lnf_sa[dc] = p.c["sa"]
                    p.wait(nc.tensor, "vc", p.c["vc"])
                    p.wait(nc.tensor, "sa", p.c["sa"])
                    mm3(nc, p, ps_r8[:], wr_h[:, dc * 8:(dc + 1) * 8],
                        wr_l[:, dc * 8:(dc + 1) * 8],
                        ln_h2[:, lsl], ln_l2[:, lsl], dc == 0, dc == NDC - 1)
                if debug and th == 0:
                    p.wait(nc.scalar, "vc")
                    p.dma(nc.scalar, d_lnh[:], ln_h2[:], "sd")
                    p.dma(nc.scalar, d_lnl[:], ln_l2[:], "sd")
                    p.wait(nc.scalar, "sd", p.c["sd"])

            def router_post(th):
                p.vdrain = True
                rtm = p.c["tm"]
                p.wait(nc.vector, "tm", rtm)
                p.v(lambda: nc.vector.tensor_copy(out=lg8[:], in_=ps_r8[:]))
                psr_free[0] = p.c["vc"]
                p.wait(nc.tensor, "vc", p.c["vc"])
                p.wait(nc.tensor, "vc", pswb_read[0])
                for tt in range(4):
                    p.inc(nc.tensor.transpose(ps_wb[:, tt * 8:(tt + 1) * 8],
                                              lg8[0:8, tt * 128:(tt + 1) * 128],
                                              ident_s[0:8, 0:8]), "tm")
                ttm = p.c["tm"]
                p.wait(nc.vector, "tm", ttm)
                p.v(lambda: nc.vector.tensor_copy(out=ltok[:, 0:4 * NEXP],
                                            in_=ps_wb[:, 0:4 * NEXP]))
                pswb_read[0] = p.c["vc"]
                for tt in range(4):
                    tsl = slice(tt * 8, (tt + 1) * 8)
                    msl = slice(tt, tt + 1)
                    p.v(lambda: nc.vector.tensor_reduce(out=m1[:, msl], in_=ltok[:, tsl],
                                                  axis=AXX, op=ALU.max))
                    p.v(lambda: nc.vector.tensor_scalar_mul(m1[:, msl], m1[:, msl], -1.0))
                    va = p.c["vc"]
                    p.wait(nc.scalar, "vc", va)
                    sa = p.s(lambda: nc.scalar.activation(out=prob[:, tsl],
                                                    in_=ltok[:, tsl],
                                                    func=AF.Exp, bias=m1[:, msl]))
                    p.wait(nc.vector, "sa", sa)
                    p.v(lambda: nc.vector.tensor_reduce(out=zr[:, msl], in_=prob[:, tsl],
                                                  axis=AXX, op=ALU.add))
                    p.v(lambda: nc.vector.reciprocal(out=zr[:, msl], in_=zr[:, msl]))
                    p.v(lambda: nc.vector.tensor_scalar(out=prob[:, tsl], in0=prob[:, tsl],
                                                  scalar1=zr[:, msl], scalar2=None,
                                                  op0=ALU.mult))
                    p.v(lambda: nc.vector.tensor_reduce(out=m1[:, msl], in_=prob[:, tsl],
                                                  axis=AXX, op=ALU.max))
                    p.v(lambda: nc.vector.tensor_scalar(out=tmp[:, tsl], in0=prob[:, tsl],
                                                  scalar1=m1[:, msl], scalar2=None,
                                                  op0=ALU.is_equal))
                    p.v(lambda: nc.vector.scalar_tensor_tensor(out=tmp[:, tsl],
                                                         in0=tmp[:, tsl],
                                                         scalar=-BIG, in1=prob[:, tsl],
                                                         op0=ALU.mult, op1=ALU.add))
                    p.v(lambda: nc.vector.tensor_reduce(out=m1[:, msl], in_=tmp[:, tsl],
                                                  axis=AXX, op=ALU.max))
                    p.v(lambda: nc.vector.tensor_scalar(out=tmp[:, tsl], in0=prob[:, tsl],
                                                  scalar1=m1[:, msl], scalar2=None,
                                                  op0=ALU.is_ge))
                    p.v(lambda: nc.vector.tensor_tensor(out=wtok[:, tsl], in0=prob[:, tsl],
                                                  in1=tmp[:, tsl], op=ALU.mult))
                p.wait(nc.tensor, "vc", p.c["vc"])
                p.wait(nc.tensor, "vc", pswb_read[0])
                for e in range(NEXP):
                    for tt in range(4):
                        p.inc(nc.tensor.transpose(
                            ps_wb[0:1, tt * 128:(tt + 1) * 128],
                            wtok[:, tt * 8 + e: tt * 8 + e + 1],
                            ident_s[:]), "tm")
                    te = p.c["tm"]
                    p.wait(nc.vector, "tm", te)
                    p.v(lambda: nc.vector.tensor_copy(
                        out=wrow[0:1, e * TP2:(e + 1) * TP2],
                        in_=ps_wb[0:1, 0:TP2]))
                    p.wait(nc.tensor, "vc", p.c["vc"])
                pswb_read[0] = p.c["vc"]
                p.vdrain = False

            tasks = [("s", 0), ("s", 11)] + [("e", e) for e in range(NEXP)]
            for th in range(2):
                items = []
                for task in tasks:
                    if task[0] == "e":
                        items.append(("wbb", task[1]))
                        we = 0
                    else:
                        we = None
                    for fc in range(11):
                        b = upi[0] % 2
                        upi[0] += 1
                        items.append(("up", task, fc, b, we))
                    for dc in range(NDC):
                        b = dni[0] % 2
                        dni[0] += 1
                        items.append(("down", task, dc, b, th))
                n = len(items)
                emit_loads((th, 0), items[0])
                emit_splits((th, 0), items[0])
                emit_loads((th, 1), items[1])
                fill_ln_router(th)
                router_post(th)
                for i in range(n):
                    if i + 1 < n:
                        emit_splits((th, i + 1), items[i + 1])
                    if i + 2 < n:
                        emit_loads((th, i + 2), items[i + 2])
                    emit_compute(items[i])
            if debug:
                p.wait(nc.scalar, "vc")
                p.dma(nc.scalar, d_x3[:], xnewT[:], "sd")
            barrier(nc, p)

        p.vdrain = True
        with ExitStack() as ph:
            Eh = ph.enter_context
            yst = [Eh(nc.sbuf_tensor(f"b5y{i}", [128, DM], F32)) for i in range(2)]
            ps_t = Eh(nc.psum_tensor("b5pt", [128, 128], F32))
            y_sd = [0, 0]
            for tt in range(8):
                i = tt % 2
                if tt >= 2:
                    p.wait(nc.vector, f"so{i}", y_sd[i])
                for dc in range(NDC):
                    t1 = p.inc(nc.tensor.transpose(
                        ps_t[:],
                        xnewT[:, dc * TPC + tt * 128: dc * TPC + (tt + 1) * 128],
                        ident_s[:]), "tm")
                    p.wait(nc.vector, "tm", t1)
                    p.v(lambda: nc.vector.tensor_copy(
                        out=yst[i][:, dc * 128:(dc + 1) * 128], in_=ps_t[:]))
                    p.wait(nc.tensor, "vc", p.c["vc"])
                p.wait(nc.scalar, "vc")
                y_sd[i] = p.dma(nc.scalar, y[tt * 128:(tt + 1) * 128, :], yst[i][:],
                                f"so{i}")
            p.wait(nc.sync, "so0", p.c["so0"])
            p.wait(nc.sync, "so1", p.c["so1"])
    return nc



def kernel(L0_attn_x, L0_attn_Wq, L0_attn_Wk, L0_attn_Wv, L0_attn_Wo,
           L0_ffn_Wr, L0_ffn_Wg, L0_ffn_Wu, L0_ffn_Wd,
           L0_ffn_Wsg, L0_ffn_Wsu, L0_ffn_Wsd):
    x = np.ascontiguousarray(np.asarray(L0_attn_x, dtype=np.float32)[0])
    xT = np.ascontiguousarray(x.T)
    Wq = np.asarray(L0_attn_Wq, dtype=np.float32)
    Wk = np.asarray(L0_attn_Wk, dtype=np.float32)
    Wv = np.asarray(L0_attn_Wv, dtype=np.float32)
    Wo = np.asarray(L0_attn_Wo, dtype=np.float32)

    mask01 = np.tril(np.ones((128, 128), dtype=np.float32), -1)
    identf = np.eye(128, dtype=np.float32)
    wgT = np.ascontiguousarray(np.asarray(L0_ffn_Wg, dtype=np.float32).transpose(0, 2, 1))
    wuT = np.ascontiguousarray(np.asarray(L0_ffn_Wu, dtype=np.float32).transpose(0, 2, 1))
    wdT = np.ascontiguousarray(np.asarray(L0_ffn_Wd, dtype=np.float32).transpose(0, 2, 1))
    wsgT = np.ascontiguousarray(np.asarray(L0_ffn_Wsg, dtype=np.float32).T)
    wsuT = np.ascontiguousarray(np.asarray(L0_ffn_Wsu, dtype=np.float32).T)
    wsdT = np.ascontiguousarray(np.asarray(L0_ffn_Wsd, dtype=np.float32).T)
    wrT = np.ascontiguousarray(np.asarray(L0_ffn_Wr, dtype=np.float32).T)

    ncF = build_full()
    in_maps = []
    for c in range(NCORE):
        csl = slice(c * HPC * HD, (c + 1) * HPC * HD)
        in_maps.append({
            "xT": xT,
            "wq": np.ascontiguousarray(Wq[:, csl]),
            "wk": np.ascontiguousarray(Wk[:, csl]),
            "wv": np.ascontiguousarray(Wv[:, csl]),
            "wo": np.ascontiguousarray(Wo[csl, :]),
            "mask01": mask01,
            "xTm": np.ascontiguousarray(xT[:, c * TPC:(c + 1) * TPC]),
            "wgT": wgT, "wuT": wuT, "wdT": wdT,
            "wsgT": wsgT, "wsuT": wsuT, "wsdT": wsdT, "wrT": wrT,
            "identf": identf,
        })
    res = run_bass_kernel_spmd(ncF, in_maps, core_ids=list(range(NCORE)))
    out = np.concatenate([res.results[c]["y"] for c in range(NCORE)], axis=0)
    return np.ascontiguousarray(out.reshape(1, S, DM), dtype=np.float32)


# version 20
# speedup vs baseline: 1.2761x; 1.0028x over previous
import math
from contextlib import ExitStack

import numpy as np
import concourse.bass as bass
import concourse.mybir as mybir
from concourse.bass_utils import run_bass_kernel_spmd

F32 = mybir.dt.float32
BF16 = mybir.dt.bfloat16
AF = mybir.ActivationFunctionType
ALU = mybir.AluOpType
AXX = mybir.AxisListType.X

S = 8192
DM = 2048
HD = 128
NCORE = 8
HPC = 2
TPC = S // NCORE
NDC = DM // 128
NT = S // 128
NTB = S // 512
FE = 1408
FS = 2816
NEXP = 8
EXPB = -8.0
SCL = 1.0 / math.sqrt(HD)
BIG = 1.0e30
EPS = 1e-5


class P:

    def __init__(self, nc, stack, names):
        self.nc = nc
        self.h = {}
        self.c = {}
        for n in names:
            self.h[n] = stack.enter_context(nc.semaphore(n))
            self.c[n] = 0

    def inc(self, instr, name, k=1):
        instr.then_inc(self.h[name], k)
        self.c[name] += k
        return self.c[name]

    def dma(self, eng, out, in_, name="ld"):
        return self.inc(eng.dma_start(out=out, in_=in_), name, 16)

    def wait(self, eng, name, val=None):
        eng.wait_ge(self.h[name], self.c[name] if val is None else val)

    vdrain = True

    def v(self, f):
        if self.vdrain:
            self.wait(self.nc.vector, "vc")
        return self.inc(f(), "vc")

    def s(self, f):
        return self.inc(f(), "sa")


def barrier(nc, p):
    engines = (nc.tensor, nc.vector, nc.scalar, nc.gpsimd, nc.sync)
    for nm in p.h:
        if nm in ("tm", "vc", "sa", "ph"):
            continue
        p.wait(nc.gpsimd, nm)
    for e in engines:
        p.inc(e.nop(), "ph", 1)
    for e in engines:
        p.wait(e, "ph")


def split_hi_lo(nc, p, src_ap, hi_ap, lo_ap):
    p.wait(nc.scalar, "vc")
    va = p.s(lambda: nc.scalar.activation(out=hi_ap, in_=src_ap, func=AF.Copy))
    p.wait(nc.vector, "sa", va)
    p.v(lambda: nc.vector.tensor_tensor(out=lo_ap, in0=src_ap, in1=hi_ap,
                                  op=ALU.subtract))


def split_w(nc, p, src_ap, hi_ap, lo_ap):
    va = p.s(lambda: nc.scalar.activation(out=hi_ap, in_=src_ap, func=AF.Copy))
    p.wait(nc.vector, "sa", va)
    p.v(lambda: nc.vector.tensor_tensor(out=lo_ap, in0=src_ap, in1=hi_ap,
                                  op=ALU.subtract))


def mm3(nc, p, out_ap, lh, ll, rh, rl, start, stop):
    p.inc(nc.tensor.matmul(out_ap, lh, rh, start=start, stop=False), "tm")
    p.inc(nc.tensor.matmul(out_ap, lh, rl, start=False, stop=False), "tm")
    p.inc(nc.tensor.matmul(out_ap, ll, rh, start=False, stop=stop), "tm")


def ln_rows(nc, p, ps_s, ps_q, trow, trow2, rrow, rstd_out, mr_out, eps_ap):
    p.v(lambda: nc.vector.tensor_scalar_mul(trow, ps_s, 1.0 / DM))
    p.v(lambda: nc.vector.tensor_scalar_mul(trow2, ps_q, 1.0 / DM))
    p.v(lambda: nc.vector.tensor_tensor(out=rrow, in0=trow, in1=trow, op=ALU.mult))
    p.v(lambda: nc.vector.tensor_tensor(out=trow2, in0=trow2, in1=rrow,
                                  op=ALU.subtract))
    va = p.c["vc"]
    p.wait(nc.scalar, "vc", va)
    sa = p.s(lambda: nc.scalar.activation(out=rrow, in_=trow2, func=AF.Sqrt,
                                    bias=eps_ap))
    p.wait(nc.vector, "sa", sa)
    p.v(lambda: nc.vector.reciprocal(out=rstd_out, in_=rrow))
    p.v(lambda: nc.vector.tensor_tensor(out=mr_out, in0=trow, in1=rstd_out,
                                  op=ALU.mult))


def build_full(debug=False):
    nc = bass.Bass("TRN2", target_bir_lowering=False, dynamic_dma_scratch_size=2048)
    dbg = dict(kind="ExternalOutput") if debug else {}
    xT = nc.dram_tensor("xT", [DM, S], F32, kind="ExternalInput").ap()
    wq = nc.dram_tensor("wq", [DM, HPC * HD], F32, kind="ExternalInput").ap()
    wk = nc.dram_tensor("wk", [DM, HPC * HD], F32, kind="ExternalInput").ap()
    wv = nc.dram_tensor("wv", [DM, HPC * HD], F32, kind="ExternalInput").ap()
    wo = nc.dram_tensor("wo", [HPC * HD, DM], F32, kind="ExternalInput").ap()
    mask01 = nc.dram_tensor("mask01", [128, 128], F32, kind="ExternalInput").ap()
    xTm = nc.dram_tensor("xTm", [DM, TPC], F32, kind="ExternalInput").ap()
    wgT = nc.dram_tensor("wgT", [NEXP, DM, FE], F32, kind="ExternalInput").ap()
    wuT = nc.dram_tensor("wuT", [NEXP, DM, FE], F32, kind="ExternalInput").ap()
    wdT = nc.dram_tensor("wdT", [NEXP, FE, DM], F32, kind="ExternalInput").ap()
    wsgT = nc.dram_tensor("wsgT", [DM, FS], F32, kind="ExternalInput").ap()
    wsuT = nc.dram_tensor("wsuT", [DM, FS], F32, kind="ExternalInput").ap()
    wsdT = nc.dram_tensor("wsdT", [FS, DM], F32, kind="ExternalInput").ap()
    wrT = nc.dram_tensor("wrT", [DM, 8], F32, kind="ExternalInput").ap()
    identf = nc.dram_tensor("identf", [128, 128], F32, kind="ExternalInput").ap()
    y = nc.dram_tensor("y", [TPC, DM], F32, kind="ExternalOutput").ap()
    rs_g = [nc.dram_tensor(f"rs_g{g}", [NCORE, 512, TPC], F32).ap()
            for g in range(4)]
    rso_g = [nc.dram_tensor(f"rso_g{g}", [512, TPC], F32).ap() for g in range(4)]

    ag_in = nc.dram_tensor("ag_in", [2, TPC], F32).ap()
    stats_g = nc.dram_tensor("stats_g", [NCORE * 2, TPC], F32).ap()
    d_xnew = nc.dram_tensor("d_xnew", [128, NDC * TPC], F32, **dbg).ap()
    d_rm = nc.dram_tensor("d_rm", [2, TPC], F32, **dbg).ap()
    d_lnh = nc.dram_tensor("d_lnh", [128, NDC * 512], BF16, **dbg).ap()
    d_lnl = nc.dram_tensor("d_lnl", [128, NDC * 512], BF16, **dbg).ap()
    d_hh = nc.dram_tensor("d_hh", [128, 11 * 512], BF16, **dbg).ap()
    d_hl = nc.dram_tensor("d_hl", [128, 11 * 512], BF16, **dbg).ap()
    d_dwh = nc.dram_tensor("d_dwh", [128, FE], BF16, **dbg).ap()
    d_dwl = nc.dram_tensor("d_dwl", [128, FE], BF16, **dbg).ap()
    d_x2 = nc.dram_tensor("d_x2", [128, NDC * TPC], F32, **dbg).ap()
    d_x3 = nc.dram_tensor("d_x3", [128, NDC * TPC], F32, **dbg).ap()
    qhi_d = nc.dram_tensor("qhi_d", [HPC, HD, S], BF16, **dbg).ap()
    qlo_d = nc.dram_tensor("qlo_d", [HPC, HD, S], BF16, **dbg).ap()
    khi_d = nc.dram_tensor("khi_d", [HPC, HD, S], BF16, **dbg).ap()
    klo_d = nc.dram_tensor("klo_d", [HPC, HD, S], BF16, **dbg).ap()
    vhi_d = nc.dram_tensor("vhi_d", [HPC, 128, S], BF16, **dbg).ap()
    vlo_d = nc.dram_tensor("vlo_d", [HPC, 128, S], BF16, **dbg).ap()

    with ExitStack() as st:
        E = st.enter_context
        p = P(nc, st, ["ld", "lq", "lq0", "lq1", "lkv", "lx0", "lx1", "lw",
                       "lp0", "lp1", "lp2", "lp3", "sd", "so0", "so1", "lg",
                       "tm", "vc", "sa", "ph", "cc"])

        p.vdrain = False
        ones_r = E(nc.sbuf_tensor("ones_r", [1, 128], F32))
        ones_c = E(nc.sbuf_tensor("ones_c", [128, 1], BF16))
        mask_s = E(nc.sbuf_tensor("mask_s", [128, 128], F32))
        eps1 = E(nc.sbuf_tensor("eps1", [1, 1], F32))
        sclc = E(nc.sbuf_tensor("sclc", [128, 1], F32))
        expbc = E(nc.sbuf_tensor("expbc", [128, 1], F32))
        ident_s = E(nc.sbuf_tensor("ident_s", [128, 128], F32))
        ascope = ExitStack()
        o_hi = [ascope.enter_context(nc.sbuf_tensor(f"o_hi{h}", [128, S], BF16))
                for h in range(HPC)]
        o_lo = [ascope.enter_context(nc.sbuf_tensor(f"o_lo{h}", [128, S], BF16))
                for h in range(HPC)]

        p.v(lambda: nc.vector.memset(ones_r[:], 1.0))
        p.v(lambda: nc.vector.memset(ones_c[:], 1.0))
        p.v(lambda: nc.vector.memset(eps1[:], EPS))
        p.v(lambda: nc.vector.memset(sclc[:], SCL))
        p.v(lambda: nc.vector.memset(expbc[:], EXPB))
        init_vc = p.c["vc"]
        p.wait(nc.scalar, "vc", init_vc)
        p.dma(nc.sync, mask_s[:], mask01)
        p.dma(nc.sync, ident_s[:], identf)

        with ExitStack() as ph:
            Eh = ph.enter_context
            xts = [Eh(nc.sbuf_tensor(f"a0x{i}", [128, 512], F32)) for i in range(2)]
            xbf = [Eh(nc.sbuf_tensor(f"a0b{i}", [128, 512], BF16)) for i in range(2)]
            sqb = [Eh(nc.sbuf_tensor(f"a0q{i}", [128, 512], BF16)) for i in range(2)]
            trow = Eh(nc.sbuf_tensor("a0t1", [1, 512], F32))
            trow2 = Eh(nc.sbuf_tensor("a0t2", [1, 512], F32))
            rrow = Eh(nc.sbuf_tensor("a0r", [1, 512], F32))
            mrow = Eh(nc.sbuf_tensor("a0m", [1, 512], F32))
            ps_s = Eh(nc.psum_tensor("a0ps", [1, 512], F32))
            ps_q = Eh(nc.psum_tensor("a0pq", [1, 512], F32))

            p.wait(nc.tensor, "vc", init_vc)
            it_tm = {}
            row_sd = 0
            for tb in range(2):
                for dc in range(NDC):
                    it = tb * NDC + dc
                    i = it % 2
                    if it >= 2:
                        p.wait(nc.sync, "tm", it_tm[it - 2])
                    v = p.dma(nc.sync, xts[i][:],
                              xTm[dc * 128:(dc + 1) * 128, tb * 512:(tb + 1) * 512],
                              f"lx{i}")
                    p.wait(nc.scalar, f"lx{i}", v)
                    if it >= 2:
                        p.wait(nc.scalar, "tm", it_tm[it - 2])
                    p.s(lambda: nc.scalar.activation(out=xbf[i][:], in_=xts[i][:],
                                               func=AF.Copy))
                    sa = p.s(lambda: nc.scalar.activation(out=sqb[i][:], in_=xts[i][:],
                                                    func=AF.Square))
                    p.wait(nc.tensor, "sa", sa)
                    p.inc(nc.tensor.matmul(ps_s[:], ones_c[:], xbf[i][:],
                                           start=(dc == 0), stop=(dc == NDC - 1)), "tm")
                    it_tm[it] = p.inc(nc.tensor.matmul(ps_q[:], ones_c[:], sqb[i][:],
                                                       start=(dc == 0),
                                                       stop=(dc == NDC - 1)), "tm")
                p.wait(nc.vector, "tm", p.c["tm"])
                if tb >= 1:
                    p.wait(nc.vector, "sd", row_sd)
                    p.wait(nc.scalar, "sd", row_sd)
                ln_rows(nc, p, ps_s[:], ps_q[:], trow[:], trow2[:], rrow[:],
                        mrow[:], trow[:], eps1[:])
                p.wait(nc.tensor, "vc", p.c["vc"])
                p.wait(nc.scalar, "vc")
                p.dma(nc.scalar, ag_in[0:1, tb * 512:(tb + 1) * 512], mrow[:], "sd")
                row_sd = p.dma(nc.scalar, ag_in[1:2, tb * 512:(tb + 1) * 512],
                               trow[:], "sd")
            p.wait(nc.gpsimd, "sd", p.c["sd"])
            cc_v = p.inc(nc.gpsimd.collective_compute(
                "AllGather", mybir.AluOpType.bypass,
                replica_groups=[list(range(NCORE))],
                ins=[ag_in[:].opt()], outs=[stats_g[:].opt()]), "cc", 1)
            p.wait(nc.gpsimd, "cc", cc_v)
            barrier(nc, p)

        with ExitStack() as ph:
            Eh = ph.enter_context
            xts = [Eh(nc.sbuf_tensor(f"a1x{i}", [128, 512], F32)) for i in range(2)]
            w_h, w_l = {}, {}
            for nm in ("q", "k", "v"):
                w_h[nm] = Eh(nc.sbuf_tensor(f"a1w{nm}h", [128, NDC * 256], BF16))
                w_l[nm] = Eh(nc.sbuf_tensor(f"a1w{nm}l", [128, NDC * 256], BF16))
            srow = Eh(nc.sbuf_tensor("a1sr", [1, 512], F32))
            mrow = Eh(nc.sbuf_tensor("a1mr", [1, 512], F32))
            r_bb = Eh(nc.sbuf_tensor("a1rb", [128, 512], F32))
            m_bb = Eh(nc.sbuf_tensor("a1mb", [128, 512], F32))
            lnf = [Eh(nc.sbuf_tensor(f"a1ln{i}", [128, 512], F32)) for i in range(2)]
            lnh = [Eh(nc.sbuf_tensor(f"a1lh{i}", [128, 512], BF16)) for i in range(2)]
            lnl = [Eh(nc.sbuf_tensor(f"a1ll{i}", [128, 512], BF16)) for i in range(2)]
            qk_st = [Eh(nc.sbuf_tensor(f"a1s{i}", [128, 512], BF16)) for i in range(8)]
            v_st = [Eh(nc.sbuf_tensor(f"a1v{i}", [128, 1024], BF16)) for i in range(2)]
            ps_qk = [Eh(nc.psum_tensor(f"a1p{i}", [128, 512], F32)) for i in range(4)]
            ps_v = [Eh(nc.psum_tensor(f"a1pv{i}", [128, 512], F32)) for i in range(2)]
            ps_bc = Eh(nc.psum_tensor("a1pb", [128, 512], F32))

            with nc.sbuf_tensor("a1wf", [128, NDC * 256], F32) as wf:
                for nm, w_ap in (("q", wq), ("k", wk), ("v", wv)):
                    v = p.dma(nc.sync, wf[:].rearrange("p (c m) -> p c m", m=256),
                              w_ap.rearrange("(c p) m -> p c m", p=128))
                    p.wait(nc.scalar, "ld", v)
                    split_hi_lo(nc, p, wf[:], w_h[nm][:], w_l[nm][:])
                    p.wait(nc.sync, "vc", p.c["vc"])

            ln_done = [0] * NTB
            dc_tm = {}
            dc_vc = {}
            qkv_sd = 0
            bc_tm = 0
            for tb in range(NTB):
                p.wait(nc.sync, "tm", bc_tm)
                gb, hb = tb // 2, (tb % 2) * 512
                p.dma(nc.sync, srow[:], stats_g[2 * gb:2 * gb + 1, hb:hb + 512], "lq")
                v = p.dma(nc.sync, mrow[:],
                          stats_g[2 * gb + 1:2 * gb + 2, hb:hb + 512], "lq")
                p.wait(nc.tensor, "lq", v)
                if tb >= 1:
                    p.wait(nc.tensor, "vc", ln_done[tb - 1])
                t1 = p.inc(nc.tensor.matmul(ps_bc[:], ones_r[:], srow[:],
                                            start=True, stop=True), "tm")
                p.wait(nc.vector, "tm", t1)
                p.v(lambda: nc.vector.tensor_copy(out=r_bb[:], in_=ps_bc[:]))
                p.wait(nc.tensor, "vc", p.c["vc"])
                t2 = p.inc(nc.tensor.matmul(ps_bc[:], ones_r[:], mrow[:],
                                            start=True, stop=True), "tm")
                bc_tm = t2
                p.wait(nc.vector, "tm", t2)
                p.v(lambda: nc.vector.tensor_copy(out=m_bb[:], in_=ps_bc[:]))

                for dc in range(NDC):
                    li = dc % 2
                    it = tb * NDC + dc
                    if it >= 2:
                        p.wait(nc.sync, "vc", dc_vc[it - 2])
                    ldv = p.dma(nc.sync, xts[li][:],
                                xT[dc * 128:(dc + 1) * 128, tb * 512:(tb + 1) * 512],
                                f"lx{li}")
                    xin = xts[li][:]
                    p.wait(nc.vector, f"lx{li}", ldv)
                    if it >= 2:
                        p.wait(nc.vector, "tm", dc_tm[it - 2])
                        p.wait(nc.scalar, "tm", dc_tm[it - 2])
                    p.v(lambda: nc.vector.tensor_tensor(out=lnf[li][:], in0=xin, in1=r_bb[:],
                                                  op=ALU.mult))
                    p.v(lambda: nc.vector.tensor_tensor(out=lnf[li][:], in0=lnf[li][:],
                                                  in1=m_bb[:], op=ALU.subtract))
                    dc_vc[it] = p.c["vc"]
                    split_hi_lo(nc, p, lnf[li][:], lnh[li][:], lnl[li][:])
                    p.wait(nc.tensor, "vc")
                    p.wait(nc.tensor, "sa")
                    first, last = dc == 0, dc == NDC - 1
                    for ti, nm in enumerate(("q", "k")):
                        for h in range(HPC):
                            lsl = slice(dc * 256 + h * 128, dc * 256 + (h + 1) * 128)
                            mm3(nc, p, ps_qk[ti * 2 + h][:], w_h[nm][:, lsl],
                                w_l[nm][:, lsl], lnh[li][:], lnl[li][:], first, last)
                    wsl = slice(dc * 256, (dc + 1) * 256)
                    for sub in range(4):
                        ssl = slice(sub * 128, (sub + 1) * 128)
                        pv = ps_v[sub // 2][:, (sub % 2) * 256:(sub % 2 + 1) * 256]
                        st0 = first and (sub % 2 == 0)
                        sp0 = last and (sub % 2 == 1)
                        p.inc(nc.tensor.matmul(pv, lnh[li][:, ssl], w_h["v"][:, wsl],
                                               start=st0, stop=False), "tm")
                        p.inc(nc.tensor.matmul(pv, lnl[li][:, ssl], w_h["v"][:, wsl],
                                               start=False, stop=False), "tm")
                        p.inc(nc.tensor.matmul(pv, lnh[li][:, ssl], w_l["v"][:, wsl],
                                               start=False, stop=sp0), "tm")
                    dc_tm[it] = p.c["tm"]
                ln_done[tb] = p.c["vc"]
                tm_qkv = p.c["tm"]

                p.wait(nc.scalar, "tm", tm_qkv)
                p.wait(nc.vector, "tm", tm_qkv)
                if tb >= 1:
                    p.wait(nc.scalar, "sd", qkv_sd)
                    p.wait(nc.vector, "sd", qkv_sd)
                for i in range(4):
                    split_hi_lo(nc, p, ps_qk[i][:], qk_st[2 * i][:], qk_st[2 * i + 1][:])
                for i in range(2):
                    split_hi_lo(nc, p, ps_v[i][:], v_st[i][:, 0:512],
                                v_st[i][:, 512:1024])
                p.wait(nc.scalar, "vc")
                tsl = slice(tb * 512, (tb + 1) * 512)
                dmap = [(qhi_d, 0, 0), (qlo_d, 0, 1), (qhi_d, 1, 2), (qlo_d, 1, 3),
                        (khi_d, 0, 4), (klo_d, 0, 5), (khi_d, 1, 6), (klo_d, 1, 7)]
                for dten, h, sti in dmap:
                    p.dma(nc.scalar, dten[h, :, tsl], qk_st[sti][:], "sd")
                for i in range(2):
                    for half, dten in ((0, vhi_d), (1, vlo_d)):
                        for h in range(HPC):
                            src = v_st[i][:].rearrange("p (a s h d) -> p a s h d",
                                                       a=2, s=2, h=2)[:, half, :, h, :]
                            k0 = tb * 4 + 2 * i
                            dst = dten[h, :, k0 * 128:(k0 + 2) * 128].rearrange(
                                "p (s d) -> p s d", d=128)
                            p.dma(nc.scalar, dst, src, "sd")
                qkv_sd = p.c["sd"]
            p.wait(nc.gpsimd, "sd", p.c["sd"])
            barrier(nc, p)

        with ExitStack() as ph:
            Eh = ph.enter_context
            k_hi = Eh(nc.sbuf_tensor("a2kh", [128, S], BF16))
            k_lo = Eh(nc.sbuf_tensor("a2kl", [128, S], BF16))
            vv_hi = Eh(nc.sbuf_tensor("a2vh", [128, S], BF16))
            vv_lo = Eh(nc.sbuf_tensor("a2vl", [128, S], BF16))
            q_hi = [Eh(nc.sbuf_tensor(f"a2qh{i}", [128, 1024], BF16)) for i in range(2)]
            q_lo = [Eh(nc.sbuf_tensor(f"a2ql{i}", [128, 1024], BF16)) for i in range(2)]
            pf = [Eh(nc.sbuf_tensor(f"a2pf{i}", [128, 1024], F32)) for i in range(2)]
            p_hi = [Eh(nc.sbuf_tensor(f"a2ph{i}", [128, 1024], BF16)) for i in range(2)]
            p_lo = [Eh(nc.sbuf_tensor(f"a2pl{i}", [128, 1024], BF16)) for i in range(2)]
            zrow = Eh(nc.sbuf_tensor("a2z", [1, 1024], F32))
            rzrow = Eh(nc.sbuf_tensor("a2rz", [1, 1024], F32))
            rz_bb = Eh(nc.sbuf_tensor("a2rzb", [128, 1024], F32))
            colv = Eh(nc.sbuf_tensor("a2cv", [128, 1], F32))
            otf = Eh(nc.sbuf_tensor("a2ot", [128, 1024], F32))
            ps_st = [Eh(nc.psum_tensor(f"a2ps{i}", [128, 1024], F32)) for i in range(2)]
            ps_ot = Eh(nc.psum_tensor("a2po", [128, 1024], F32))
            ps_z = Eh(nc.psum_tensor("a2pz", [1, 1024], F32))

            gq_tm = {}
            gq = 0
            kv_tm = 0
            for h in range(HPC):
                p.wait(nc.sync, "tm", kv_tm)
                p.dma(nc.sync, k_hi[:], khi_d[h], "lkv")
                p.dma(nc.sync, k_lo[:], klo_d[h], "lkv")
                p.dma(nc.sync, vv_hi[:], vhi_d[h], "lkv")
                lk = p.dma(nc.sync, vv_lo[:], vlo_d[h], "lkv")
                p.wait(nc.tensor, "lkv", lk)

                for kt in range(NT):
                    p.inc(nc.tensor.matmul(ps_st[0][:, 0:1],
                                           vv_hi[:, kt * 128:(kt + 1) * 128], ones_c[:],
                                           start=(kt == 0), stop=False), "tm")
                    p.inc(nc.tensor.matmul(ps_st[0][:, 0:1],
                                           vv_lo[:, kt * 128:(kt + 1) * 128], ones_c[:],
                                           start=False, stop=(kt == NT - 1)), "tm")
                p.wait(nc.vector, "tm", p.c["tm"])
                p.v(lambda: nc.vector.tensor_copy(out=colv[:], in_=ps_st[0][:, 0:1]))
                p.wait(nc.tensor, "vc", p.c["vc"])

                for qg in range(8):
                    qb = gq % 2
                    qsl = slice(qg * 1024, (qg + 1) * 1024)
                    if gq >= 2:
                        p.wait(nc.sync, "tm", gq_tm[gq - 2])
                    p.dma(nc.sync, q_hi[qb][:], qhi_d[h, :, qsl], f"lq{qb}")
                    lq2 = p.dma(nc.sync, q_lo[qb][:], qlo_d[h, :, qsl], f"lq{qb}")
                    p.wait(nc.tensor, f"lq{qb}", lq2)

                    kts = list(range(8 * qg, NT))
                    nk = len(kts)
                    exp_done, split_done, vz_done = {}, {}, {}
                    sc_tm = {}

                    def emit_scores(j):
                        ktj = kts[j]
                        ncj = 128 * (min(ktj, 8 * qg + 7) - 8 * qg + 1)
                        sbj = j % 2
                        kslj = slice(ktj * 128, (ktj + 1) * 128)
                        if j >= 2:
                            p.wait(nc.tensor, "sa", exp_done[j - 2])
                        for c0 in range(0, ncj, 512):
                            c1 = min(c0 + 512, ncj)
                            mm3(nc, p, ps_st[sbj][:, c0:c1], k_hi[:, kslj],
                                k_lo[:, kslj], q_hi[qb][:, c0:c1],
                                q_lo[qb][:, c0:c1], True, True)
                            sc_tm[(j, c0)] = p.c["tm"]

                    emit_scores(0)
                    for i, kt in enumerate(kts):
                        ncols = 128 * (min(kt, 8 * qg + 7) - 8 * qg + 1)
                        sb = i % 2
                        ksl = slice(kt * 128, (kt + 1) * 128)
                        if i + 1 < nk:
                            emit_scores(i + 1)
                        fin = kt == NT - 1
                        first_chunk = True
                        for e0 in range(0, ncols, 512):
                            e1 = min(e0 + 512, ncols)
                            p.wait(nc.scalar, "tm", sc_tm[(i, e0)])
                            if first_chunk and i >= 2:
                                p.wait(nc.scalar, "vc", split_done[i - 2])
                                p.wait(nc.scalar, "tm", vz_done[i - 2])
                            sa = p.s(lambda: nc.scalar.activation(
                                out=pf[sb][:, e0:e1], in_=ps_st[sb][:, e0:e1],
                                func=AF.Exp, bias=expbc[:], scale=sclc[:]))
                            p.wait(nc.vector, "sa", sa)
                            if first_chunk and i >= 2:
                                p.wait(nc.vector, "tm", vz_done[i - 2])
                            first_chunk = False
                            diag_here = kt <= 8 * qg + 7 and e1 == ncols
                            if diag_here:
                                p.v(lambda: nc.vector.tensor_tensor(
                                    out=pf[sb][:, ncols - 128:ncols],
                                    in0=pf[sb][:, ncols - 128:ncols],
                                    in1=mask_s[:], op=ALU.mult))
                            p.v(lambda: nc.vector.tensor_copy(
                                out=p_hi[sb][:, e0:e1], in_=pf[sb][:, e0:e1]))
                            p.v(lambda: nc.vector.tensor_tensor(
                                out=p_lo[sb][:, e0:e1], in0=pf[sb][:, e0:e1],
                                in1=p_hi[sb][:, e0:e1], op=ALU.subtract))
                            ck_vc = p.c["vc"]
                            p.wait(nc.tensor, "vc", ck_vc)
                            cuts = {e0, e1}
                            if diag_here and ncols - 128 > e0:
                                cuts.add(ncols - 128)
                            cuts = sorted(cuts)
                            for c0, c1 in zip(cuts[:-1], cuts[1:]):
                                stt = (c0 == 0 and kt == 8 * qg) or \
                                      (c0 == 512 and kt == 8 * qg + 4)
                                seg_fin = fin and c1 == e1
                                p.inc(nc.tensor.matmul(ps_ot[:, c0:c1], vv_hi[:, ksl],
                                                       p_hi[sb][:, c0:c1],
                                                       start=stt, stop=False), "tm")
                                p.inc(nc.tensor.matmul(ps_ot[:, c0:c1], vv_hi[:, ksl],
                                                       p_lo[sb][:, c0:c1],
                                                       start=False, stop=False), "tm")
                                p.inc(nc.tensor.matmul(ps_ot[:, c0:c1], vv_lo[:, ksl],
                                                       p_hi[sb][:, c0:c1],
                                                       start=False, stop=seg_fin),
                                      "tm")
                                p.inc(nc.tensor.matmul(ps_z[:, c0:c1], ones_c[:],
                                                       p_hi[sb][:, c0:c1],
                                                       start=stt, stop=False), "tm")
                                p.inc(nc.tensor.matmul(ps_z[:, c0:c1], ones_c[:],
                                                       p_lo[sb][:, c0:c1],
                                                       start=False, stop=seg_fin),
                                      "tm")
                        exp_done[i] = p.c["sa"]
                        split_done[i] = p.c["vc"]
                        vz_done[i] = p.c["tm"]
                    gq_tm[gq] = p.c["tm"]
                    av_tm = p.c["tm"]

                    p.wait(nc.vector, "tm", av_tm)
                    p.v(lambda: nc.vector.tensor_copy(out=zrow[:], in_=ps_z[:]))
                    if qg == 7:
                        p.v(lambda: nc.vector.memset(zrow[:, 1023:1024], 1.0))
                    p.v(lambda: nc.vector.reciprocal(out=rzrow[:], in_=zrow[:]))
                    p.wait(nc.tensor, "vc", p.c["vc"])
                    p.inc(nc.tensor.matmul(ps_st[0][:, 0:512], ones_r[:],
                                           rzrow[:, 0:512], start=True, stop=True),
                          "tm")
                    t2 = p.inc(nc.tensor.matmul(ps_st[0][:, 512:1024], ones_r[:],
                                                rzrow[:, 512:1024], start=True,
                                                stop=True), "tm")
                    p.wait(nc.vector, "tm", t2)
                    p.v(lambda: nc.vector.tensor_copy(out=rz_bb[:], in_=ps_st[0][:]))
                    p.v(lambda: nc.vector.tensor_tensor(out=otf[:], in0=ps_ot[:],
                                                  in1=rz_bb[:], op=ALU.mult))
                    if qg == 7:
                        p.v(lambda: nc.vector.tensor_scalar_mul(otf[:, 1023:1024], colv[:],
                                                          1.0 / S))
                    split_hi_lo(nc, p, otf[:], o_hi[h][:, qsl], o_lo[h][:, qsl])
                    p.wait(nc.tensor, "vc", p.c["vc"])
                    gq += 1
                kv_tm = p.c["tm"]
            barrier(nc, p)
        p.vdrain = True

        with ExitStack() as ph:
            Eh = ph.enter_context
            wof = Eh(nc.sbuf_tensor("a3wf", [128, HPC * DM], F32))
            wo_h = Eh(nc.sbuf_tensor("a3wh", [128, HPC * DM], BF16))
            wo_l = Eh(nc.sbuf_tensor("a3wl", [128, HPC * DM], BF16))
            ost = [Eh(nc.sbuf_tensor(f"a3o{i}", [128, 1024], F32)) for i in range(2)]
            ps_o = [Eh(nc.psum_tensor(f"a3p{i}", [128, 1024], F32)) for i in range(2)]

            v = p.dma(nc.sync, wof[:].rearrange("p (h n) -> p h n", h=HPC),
                      wo.rearrange("(h p) n -> p h n", p=128))
            p.wait(nc.scalar, "ld", v)
            split_hi_lo(nc, p, wof[:], wo_h[:], wo_l[:])
            p.wait(nc.tensor, "vc", p.c["vc"])
            p.wait(nc.tensor, "sa", p.c["sa"])

            it_sa = {}
            it_sd = {}
            cc_at = {}
            it = 0
            for g in range(4):
                for d2 in range(g * 4, g * 4 + 4):
                    for tc in range(NCORE):
                        tsl = slice(tc * 1024, (tc + 1) * 1024)
                        i = it % 2
                        if it >= 2:
                            p.wait(nc.tensor, "sa", it_sa[it - 2])
                        for c0 in (0, 512):
                            csl = slice(c0, c0 + 512)
                            for hh in range(HPC):
                                lsl = slice(hh * DM + d2 * 128,
                                            hh * DM + (d2 + 1) * 128)
                                mm3(nc, p, ps_o[i][:, csl], wo_h[:, lsl],
                                    wo_l[:, lsl], o_hi[hh][:, tsl][:, csl],
                                    o_lo[hh][:, tsl][:, csl],
                                    start=(hh == 0), stop=(hh == HPC - 1))
                        tmv = p.c["tm"]
                        p.wait(nc.scalar, "tm", tmv)
                        if it >= 2:
                            p.wait(nc.scalar, f"so{i}", it_sd[it - 2])
                        it_sa[it] = p.s(lambda: nc.scalar.activation(
                            out=ost[i][:], in_=ps_o[i][:], func=AF.Copy))
                        it_sd[it] = p.dma(
                            nc.scalar,
                            rs_g[g][tc, (d2 % 4) * 128:(d2 % 4 + 1) * 128, :],
                            ost[i][:], f"so{i}")
                        it += 1
                if g < 3:
                    p.wait(nc.gpsimd, "so0", p.c["so0"])
                    p.wait(nc.gpsimd, "so1", p.c["so1"])
                    cc_at[g] = p.inc(nc.gpsimd.collective_compute(
                        "ReduceScatter", mybir.AluOpType.add,
                        replica_groups=[list(range(NCORE))],
                        ins=[rs_g[g][:].opt()], outs=[rso_g[g][:].opt()]), "cc", 1)
        ascope.close()
        barrier(nc, p)
        cc_at[3] = p.inc(nc.gpsimd.collective_compute(
            "ReduceScatter", mybir.AluOpType.add,
            replica_groups=[list(range(NCORE))],
            ins=[rs_g[3][:].opt()], outs=[rso_g[3][:].opt()]), "cc", 1)
        xnewT = E(nc.sbuf_tensor("xnewT", [128, NDC * TPC], F32))
        rstd = E(nc.sbuf_tensor("b1rs", [1, TPC], F32))
        mr = E(nc.sbuf_tensor("b1mrr", [1, TPC], F32))

        with ExitStack() as ph:
            Eh = ph.enter_context
            pin = [Eh(nc.sbuf_tensor(f"b0p{i}", [128, TPC], F32)) for i in range(4)]
            done_vc = {}
            for dc in range(NDC):
                dsl = slice((dc % 4) * 128, (dc % 4 + 1) * 128)
                xsl = slice(dc * TPC, (dc + 1) * TPC)
                i0, i1 = (dc % 2) * 2, (dc % 2) * 2 + 1
                if dc == 12:
                    p.wait(nc.sync, "cc", cc_at[3])
                if dc >= 2:
                    p.wait(nc.sync, "vc", done_vc[dc - 2])
                p.dma(nc.sync, pin[i0][:], rso_g[dc // 4][dsl, :], f"lp{i0}")
                v1 = p.dma(nc.sync, pin[i1][:], xTm[dc * 128:(dc + 1) * 128, :],
                           f"lp{i1}")
                p.wait(nc.vector, f"lp{i0}", p.c[f"lp{i0}"])
                p.wait(nc.vector, f"lp{i1}", v1)
                p.v(lambda: nc.vector.tensor_tensor(out=xnewT[:, xsl], in0=pin[i0][:],
                                                    in1=pin[i1][:], op=ALU.add))
                done_vc[dc] = p.c["vc"]
            barrier(nc, p)
        with ExitStack() as ph:
            Eh = ph.enter_context
            xbf = [Eh(nc.sbuf_tensor(f"b1xb{i}", [128, TPC], BF16)) for i in range(2)]
            sqb = [Eh(nc.sbuf_tensor(f"b1sq{i}", [128, TPC], BF16)) for i in range(2)]
            trow = Eh(nc.sbuf_tensor("b1t1", [1, TPC], F32))
            trow2 = Eh(nc.sbuf_tensor("b1t2", [1, TPC], F32))
            rrow = Eh(nc.sbuf_tensor("b1r", [1, TPC], F32))
            ps_s = Eh(nc.psum_tensor("b1ps", [1, TPC], F32))
            ps_q = Eh(nc.psum_tensor("b1pq", [1, TPC], F32))
            ones_cb = Eh(nc.sbuf_tensor("b1oc", [128, 1], BF16))
            p.v(lambda: nc.vector.memset(ones_cb[:], 1.0))
            oc_vc = p.c["vc"]

            it_tm = {}
            p.wait(nc.tensor, "vc", oc_vc)
            for dc in range(NDC):
                i = dc % 2
                xin = xnewT[:, dc * TPC:(dc + 1) * TPC]
                if dc >= 2:
                    p.wait(nc.scalar, "tm", it_tm[dc - 2])
                p.s(lambda: nc.scalar.activation(out=xbf[i][:], in_=xin, func=AF.Copy))
                sa = p.s(lambda: nc.scalar.activation(out=sqb[i][:], in_=xin,
                                                func=AF.Square))
                p.wait(nc.tensor, "sa", sa)
                for c0 in (0, 512):
                    csl = slice(c0, c0 + 512)
                    p.inc(nc.tensor.matmul(ps_s[:, csl], ones_cb[:], xbf[i][:, csl],
                                           start=(dc == 0), stop=(dc == NDC - 1)), "tm")
                    p.inc(nc.tensor.matmul(ps_q[:, csl], ones_cb[:], sqb[i][:, csl],
                                           start=(dc == 0), stop=(dc == NDC - 1)), "tm")
                it_tm[dc] = p.c["tm"]
            p.wait(nc.vector, "tm", p.c["tm"])
            ln_rows(nc, p, ps_s[:], ps_q[:], trow[:], trow2[:], rrow[:],
                    rstd[:], mr[:], eps1[:])
            if debug:
                p.wait(nc.scalar, "vc")
                p.dma(nc.scalar, d_rm[0:1, :], rstd[:], "sd")
                p.dma(nc.scalar, d_rm[1:2, :], mr[:], "sd")
                p.dma(nc.scalar, d_xnew[:], xnewT[:], "sd")
            barrier(nc, p)

        TP2 = 512
        p.vdrain = False
        with ExitStack() as ph:
            Eh = ph.enter_context
            ln_h2 = Eh(nc.sbuf_tensor("b4lnh", [128, NDC * TP2], BF16))
            ln_l2 = Eh(nc.sbuf_tensor("b4lnl", [128, NDC * TP2], BF16))
            lnf2 = [Eh(nc.sbuf_tensor(f"b4lf{i}", [128, TP2], F32)) for i in range(2)]
            r_bb = Eh(nc.sbuf_tensor("b4rb", [128, TP2], F32))
            m_bb = Eh(nc.sbuf_tensor("b4mb", [128, TP2], F32))
            ust = [Eh(nc.sbuf_tensor(f"b4us{i}", [128, DM], F32)) for i in range(2)]
            up_h = [[Eh(nc.sbuf_tensor(f"b4u{k}h{b}", [128, DM], BF16))
                     for b in range(2)] for k in range(2)]
            up_l = [[Eh(nc.sbuf_tensor(f"b4u{k}l{b}", [128, DM], BF16))
                     for b in range(2)] for k in range(2)]
            dn_h = [Eh(nc.sbuf_tensor(f"b4dh{i}", [128, FE], BF16)) for i in range(2)]
            dn_l = [Eh(nc.sbuf_tensor(f"b4dl{i}", [128, FE], BF16)) for i in range(2)]
            h_hi = Eh(nc.sbuf_tensor("b4hh", [128, 11 * TP2], BF16))
            h_lo = Eh(nc.sbuf_tensor("b4hl", [128, 11 * TP2], BF16))
            hf = Eh(nc.sbuf_tensor("b4hf", [128, TP2], F32))
            w_bb = Eh(nc.sbuf_tensor("b4wb", [128, TP2], F32))
            wrow = Eh(nc.sbuf_tensor("b4wro", [1, NEXP * TP2], F32))
            wr_h = Eh(nc.sbuf_tensor("b4wrh", [128, NDC * 8], BF16))
            wr_l = Eh(nc.sbuf_tensor("b4wrl", [128, NDC * 8], BF16))
            lg8 = Eh(nc.sbuf_tensor("b4lg8", [8, TP2], F32))
            ltok = Eh(nc.sbuf_tensor("b4lt", [128, 4 * NEXP], F32))
            m1 = Eh(nc.sbuf_tensor("b4m1", [128, 4], F32))
            zr = Eh(nc.sbuf_tensor("b4zr", [128, 4], F32))
            prob = Eh(nc.sbuf_tensor("b4p", [128, 4 * NEXP], F32))
            tmp = Eh(nc.sbuf_tensor("b4tm2", [128, 4 * NEXP], F32))
            wtok = Eh(nc.sbuf_tensor("b4wt", [128, 4 * NEXP], F32))
            ps_g = [Eh(nc.psum_tensor(f"b4pg{i}", [128, TP2], F32)) for i in range(2)]
            ps_u = [Eh(nc.psum_tensor(f"b4pu{i}", [128, TP2], F32)) for i in range(2)]
            ps_o = [Eh(nc.psum_tensor(f"b4po{i}", [128, TP2], F32)) for i in range(2)]
            ps_wb = Eh(nc.psum_tensor("b4pw", [128, TP2], F32))
            ps_r8 = Eh(nc.psum_tensor("b4pr", [8, TP2], F32))

            v = p.dma(nc.sync, ust[0][:, 0:NDC * 8].rearrange("p (c e) -> p c e", e=8),
                      wrT.rearrange("(c p) e -> p c e", p=128), "lw")
            p.wait(nc.scalar, "lw", v)
            split_w(nc, p, ust[0][:, 0:NDC * 8], wr_h[:], wr_l[:])

            stq = [0]
            stage_free = [p.c["vc"], p.c["vc"]]
            up_used_tm = [0, 0]
            up_split_vc = [0, 0]
            up_split_sa = [0, 0]
            silu_sa = [0, 0]
            psu_read_vc = [0, 0]
            dn_used_tm = [0, 0]
            dn_split_vc = [0, 0]
            dn_split_sa = [0, 0]
            xadd_vc = [0, 0]
            wbb_read_vc = [0]
            pswb_read = [0]
            psr_free = [0]
            h_done_vc = [0]
            h_done_sa = [0]
            upi = [0]
            dni = [0]
            item_ld = {}
            dbg_h = [False]
            dbg_d = [False]
            dbg_x2 = [False]

            def up_srcs(task, fc):
                kind, arg = task
                if kind == "s":
                    gfc = arg + fc
                    bases = (wsgT, wsuT)
                    return [b.rearrange("(c p) f -> p c f", p=128)[
                        :, :, gfc * 128:(gfc + 1) * 128] for b in bases]
                return [b[arg].rearrange("(c p) f -> p c f", p=128)[
                    :, :, fc * 128:(fc + 1) * 128] for b in (wgT, wuT)]

            def emit_loads(key, item):
                if item[0] == "up":
                    _, task, fc, b, we = item
                    lands, stages = [], []
                    for src in up_srcs(task, fc):
                        s = stq[0] % 2
                        stq[0] += 1
                        p.wait(nc.sync, "vc", stage_free[s])
                        lands.append(p.dma(
                            nc.sync,
                            ust[s][:].rearrange("p (c f) -> p c f", f=128),
                            src, "lw"))
                        stages.append(s)
                    item_ld[key] = (stages, lands)
                elif item[0] == "down":
                    _, task, dc, b, th = item
                    kind, arg = task
                    s = stq[0] % 2
                    stq[0] += 1
                    dsl = slice(dc * 128, (dc + 1) * 128)
                    if kind == "s":
                        src = wsdT.rearrange("(c p) d -> p c d", p=128)[
                            :, arg:arg + 11, dsl]
                    else:
                        src = wdT[arg].rearrange("(c p) d -> p c d", p=128)[:, :, dsl]
                    p.wait(nc.sync, "vc", stage_free[s])
                    land = p.dma(nc.sync,
                                 ust[s][:, 0:FE].rearrange("p (c d) -> p c d", d=128),
                                 src, "lg")
                    item_ld[key] = ([s], [land])

            def emit_splits(key, item):
                if item[0] == "up":
                    _, task, fc, b, we = item
                    stages, lands = item_ld[key]
                    p.wait(nc.scalar, "tm", up_used_tm[b])
                    for j in range(2):
                        s = stages[j]
                        p.wait(nc.scalar, "lw", lands[j])
                        split_w(nc, p, ust[s][:], up_h[j][b][:], up_l[j][b][:])
                        stage_free[s] = p.c["vc"]
                    up_split_vc[b] = p.c["vc"]
                    up_split_sa[b] = p.c["sa"]
                elif item[0] == "down":
                    _, task, dc, b, th = item
                    (s,), (land,) = item_ld[key]
                    p.wait(nc.scalar, "lg", land)
                    p.wait(nc.scalar, "tm", dn_used_tm[b])
                    split_w(nc, p, ust[s][:, 0:FE], dn_h[b][:], dn_l[b][:])
                    stage_free[s] = p.c["vc"]
                    dn_split_vc[b] = p.c["vc"]
                    dn_split_sa[b] = p.c["sa"]

            def emit_compute(item):
                if item[0] == "up":
                    _, task, fc, b, we = item
                    p.wait(nc.tensor, "vc", up_split_vc[b])
                    p.wait(nc.tensor, "sa", up_split_sa[b])
                    p.wait(nc.tensor, "sa", silu_sa[b])
                    p.wait(nc.tensor, "vc", psu_read_vc[b])
                    for dc in range(NDC):
                        wsl = slice(dc * 128, (dc + 1) * 128)
                        lsl = slice(dc * TP2, (dc + 1) * TP2)
                        first, last = dc == 0, dc == NDC - 1
                        mm3(nc, p, ps_g[b][:], up_h[0][b][:, wsl], up_l[0][b][:, wsl],
                            ln_h2[:, lsl], ln_l2[:, lsl], first, last)
                        mm3(nc, p, ps_u[b][:], up_h[1][b][:, wsl], up_l[1][b][:, wsl],
                            ln_h2[:, lsl], ln_l2[:, lsl], first, last)
                    up_used_tm[b] = p.c["tm"]
                    utm = p.c["tm"]
                    p.wait(nc.scalar, "tm", utm)
                    p.wait(nc.scalar, "vc", h_done_vc[0])
                    sa = p.s(lambda: nc.scalar.activation(out=hf[:], in_=ps_g[b][:],
                                                    func=AF.Silu))
                    silu_sa[b] = sa
                    p.wait(nc.vector, "sa", sa)
                    p.wait(nc.vector, "tm", utm)
                    p.v(lambda: nc.vector.tensor_tensor(out=hf[:], in0=hf[:],
                                                  in1=ps_u[b][:], op=ALU.mult))
                    if we is not None:
                        p.v(lambda: nc.vector.tensor_tensor(out=hf[:], in0=hf[:],
                                                      in1=w_bb[:], op=ALU.mult))
                        wbb_read_vc[0] = p.c["vc"]
                    psu_read_vc[b] = p.c["vc"]
                    hsl = slice(fc * TP2, (fc + 1) * TP2)
                    split_hi_lo(nc, p, hf[:], h_hi[:, hsl], h_lo[:, hsl])
                    h_done_vc[0] = p.c["vc"]
                    h_done_sa[0] = p.c["sa"]
                    if debug and task == ("s", 0) and fc == 10 and not dbg_h[0]:
                        dbg_h[0] = True
                        p.wait(nc.scalar, "vc")
                        p.dma(nc.scalar, d_hh[:], h_hi[:], "sd")
                        p.dma(nc.scalar, d_hl[:], h_lo[:], "sd")
                        p.wait(nc.scalar, "sd", p.c["sd"])
                elif item[0] == "down":
                    _, task, dc, b, th = item
                    p.wait(nc.tensor, "vc", dn_split_vc[b])
                    p.wait(nc.tensor, "sa", dn_split_sa[b])
                    p.wait(nc.tensor, "vc", xadd_vc[b])
                    if dc == 0:
                        p.wait(nc.tensor, "vc", h_done_vc[0])
                        p.wait(nc.tensor, "sa", h_done_sa[0])
                    for fc in range(11):
                        fsl = slice(fc * TP2, (fc + 1) * TP2)
                        mm3(nc, p, ps_o[b][:], dn_h[b][:, fc * 128:(fc + 1) * 128],
                            dn_l[b][:, fc * 128:(fc + 1) * 128],
                            h_hi[:, fsl], h_lo[:, fsl], fc == 0, fc == 10)
                    dn_used_tm[b] = p.c["tm"]
                    dtm = p.c["tm"]
                    p.wait(nc.vector, "tm", dtm)
                    xsl = slice(dc * TPC + th * TP2, dc * TPC + th * TP2 + TP2)
                    p.v(lambda: nc.vector.tensor_tensor(out=xnewT[:, xsl],
                                                  in0=xnewT[:, xsl],
                                                  in1=ps_o[b][:], op=ALU.add))
                    xadd_vc[b] = p.c["vc"]
                    if debug and task == ("s", 0) and dc == 1 and not dbg_d[0]:
                        dbg_d[0] = True
                        p.wait(nc.scalar, "sa", dn_split_sa[b])
                        p.wait(nc.scalar, "vc", dn_split_vc[b])
                        p.dma(nc.scalar, d_dwh[:], dn_h[b][:], "sd")
                        p.dma(nc.scalar, d_dwl[:], dn_l[b][:], "sd")
                        p.wait(nc.scalar, "sd", p.c["sd"])
                    if debug and task == ("s", 0) and dc == NDC - 1 and not dbg_x2[0]:
                        dbg_x2[0] = True
                        p.wait(nc.scalar, "vc")
                        p.dma(nc.scalar, d_x2[:], xnewT[:], "sd")
                        p.wait(nc.scalar, "sd", p.c["sd"])
                else:
                    _, e = item[0], item[1]
                    p.wait(nc.tensor, "vc", wbb_read_vc[0])
                    p.wait(nc.tensor, "vc", pswb_read[0])
                    t = p.inc(nc.tensor.matmul(ps_wb[:], ones_r[:],
                                               wrow[0:1, e * TP2:(e + 1) * TP2],
                                               start=True, stop=True), "tm")
                    p.wait(nc.vector, "tm", t)
                    p.v(lambda: nc.vector.tensor_copy(out=w_bb[:], in_=ps_wb[:]))
                    pswb_read[0] = p.c["vc"]

            def fill_ln_router(th):
                gate_tm = p.c["tm"]
                p.wait(nc.vector, "tm", gate_tm)
                p.wait(nc.scalar, "tm", gate_tm)
                p.wait(nc.tensor, "vc", pswb_read[0])
                tsl = slice(th * TP2, (th + 1) * TP2)
                t1 = p.inc(nc.tensor.matmul(ps_wb[:], ones_r[:], rstd[0:1, tsl],
                                            start=True, stop=True), "tm")
                p.wait(nc.vector, "tm", t1)
                p.v(lambda: nc.vector.tensor_copy(out=r_bb[:], in_=ps_wb[:]))
                p.wait(nc.tensor, "vc", p.c["vc"])
                t2 = p.inc(nc.tensor.matmul(ps_wb[:], ones_r[:], mr[0:1, tsl],
                                            start=True, stop=True), "tm")
                p.wait(nc.vector, "tm", t2)
                p.v(lambda: nc.vector.tensor_copy(out=m_bb[:], in_=ps_wb[:]))
                pswb_read[0] = p.c["vc"]
                p.wait(nc.tensor, "vc", psr_free[0])
                lnf_sa = {}
                for dc in range(NDC):
                    i = dc % 2
                    xin = xnewT[:, dc * TPC + th * TP2: dc * TPC + th * TP2 + TP2]
                    lsl = slice(dc * TP2, (dc + 1) * TP2)
                    if dc >= 2:
                        p.wait(nc.vector, "sa", lnf_sa[dc - 2])
                    p.v(lambda: nc.vector.tensor_tensor(out=lnf2[i][:], in0=xin,
                                                  in1=r_bb[:], op=ALU.mult))
                    p.v(lambda: nc.vector.tensor_tensor(out=lnf2[i][:], in0=lnf2[i][:],
                                                  in1=m_bb[:], op=ALU.subtract))
                    split_hi_lo(nc, p, lnf2[i][:], ln_h2[:, lsl], ln_l2[:, lsl])
                    lnf_sa[dc] = p.c["sa"]
                    p.wait(nc.tensor, "vc", p.c["vc"])
                    p.wait(nc.tensor, "sa", p.c["sa"])
                    mm3(nc, p, ps_r8[:], wr_h[:, dc * 8:(dc + 1) * 8],
                        wr_l[:, dc * 8:(dc + 1) * 8],
                        ln_h2[:, lsl], ln_l2[:, lsl], dc == 0, dc == NDC - 1)
                if debug and th == 0:
                    p.wait(nc.scalar, "vc")
                    p.dma(nc.scalar, d_lnh[:], ln_h2[:], "sd")
                    p.dma(nc.scalar, d_lnl[:], ln_l2[:], "sd")
                    p.wait(nc.scalar, "sd", p.c["sd"])

            def router_post(th):
                p.vdrain = True
                rtm = p.c["tm"]
                p.wait(nc.vector, "tm", rtm)
                p.v(lambda: nc.vector.tensor_copy(out=lg8[:], in_=ps_r8[:]))
                psr_free[0] = p.c["vc"]
                p.wait(nc.tensor, "vc", p.c["vc"])
                p.wait(nc.tensor, "vc", pswb_read[0])
                for tt in range(4):
                    p.inc(nc.tensor.transpose(ps_wb[:, tt * 8:(tt + 1) * 8],
                                              lg8[0:8, tt * 128:(tt + 1) * 128],
                                              ident_s[0:8, 0:8]), "tm")
                ttm = p.c["tm"]
                p.wait(nc.vector, "tm", ttm)
                p.v(lambda: nc.vector.tensor_copy(out=ltok[:, 0:4 * NEXP],
                                            in_=ps_wb[:, 0:4 * NEXP]))
                pswb_read[0] = p.c["vc"]
                for tt in range(4):
                    tsl = slice(tt * 8, (tt + 1) * 8)
                    msl = slice(tt, tt + 1)
                    p.v(lambda: nc.vector.tensor_reduce(out=m1[:, msl], in_=ltok[:, tsl],
                                                  axis=AXX, op=ALU.max))
                    p.v(lambda: nc.vector.tensor_scalar_mul(m1[:, msl], m1[:, msl], -1.0))
                    va = p.c["vc"]
                    p.wait(nc.scalar, "vc", va)
                    sa = p.s(lambda: nc.scalar.activation(out=prob[:, tsl],
                                                    in_=ltok[:, tsl],
                                                    func=AF.Exp, bias=m1[:, msl]))
                    p.wait(nc.vector, "sa", sa)
                    p.v(lambda: nc.vector.tensor_reduce(out=zr[:, msl], in_=prob[:, tsl],
                                                  axis=AXX, op=ALU.add))
                    p.v(lambda: nc.vector.reciprocal(out=zr[:, msl], in_=zr[:, msl]))
                    p.v(lambda: nc.vector.tensor_scalar(out=prob[:, tsl], in0=prob[:, tsl],
                                                  scalar1=zr[:, msl], scalar2=None,
                                                  op0=ALU.mult))
                    p.v(lambda: nc.vector.tensor_reduce(out=m1[:, msl], in_=prob[:, tsl],
                                                  axis=AXX, op=ALU.max))
                    p.v(lambda: nc.vector.tensor_scalar(out=tmp[:, tsl], in0=prob[:, tsl],
                                                  scalar1=m1[:, msl], scalar2=None,
                                                  op0=ALU.is_equal))
                    p.v(lambda: nc.vector.scalar_tensor_tensor(out=tmp[:, tsl],
                                                         in0=tmp[:, tsl],
                                                         scalar=-BIG, in1=prob[:, tsl],
                                                         op0=ALU.mult, op1=ALU.add))
                    p.v(lambda: nc.vector.tensor_reduce(out=m1[:, msl], in_=tmp[:, tsl],
                                                  axis=AXX, op=ALU.max))
                    p.v(lambda: nc.vector.tensor_scalar(out=tmp[:, tsl], in0=prob[:, tsl],
                                                  scalar1=m1[:, msl], scalar2=None,
                                                  op0=ALU.is_ge))
                    p.v(lambda: nc.vector.tensor_tensor(out=wtok[:, tsl], in0=prob[:, tsl],
                                                  in1=tmp[:, tsl], op=ALU.mult))
                p.wait(nc.tensor, "vc", p.c["vc"])
                p.wait(nc.tensor, "vc", pswb_read[0])
                for e in range(NEXP):
                    for tt in range(4):
                        p.inc(nc.tensor.transpose(
                            ps_wb[0:1, tt * 128:(tt + 1) * 128],
                            wtok[:, tt * 8 + e: tt * 8 + e + 1],
                            ident_s[:]), "tm")
                    te = p.c["tm"]
                    p.wait(nc.vector, "tm", te)
                    p.v(lambda: nc.vector.tensor_copy(
                        out=wrow[0:1, e * TP2:(e + 1) * TP2],
                        in_=ps_wb[0:1, 0:TP2]))
                    p.wait(nc.tensor, "vc", p.c["vc"])
                pswb_read[0] = p.c["vc"]
                p.vdrain = False

            tasks = [("s", 0), ("s", 11)] + [("e", e) for e in range(NEXP)]
            for th in range(2):
                items = []
                for task in tasks:
                    if task[0] == "e":
                        items.append(("wbb", task[1]))
                        we = 0
                    else:
                        we = None
                    for fc in range(11):
                        b = upi[0] % 2
                        upi[0] += 1
                        items.append(("up", task, fc, b, we))
                    for dc in range(NDC):
                        b = dni[0] % 2
                        dni[0] += 1
                        items.append(("down", task, dc, b, th))
                n = len(items)
                emit_loads((th, 0), items[0])
                emit_splits((th, 0), items[0])
                emit_loads((th, 1), items[1])
                fill_ln_router(th)
                router_post(th)
                for i in range(n):
                    if i + 1 < n:
                        emit_splits((th, i + 1), items[i + 1])
                    if i + 2 < n:
                        emit_loads((th, i + 2), items[i + 2])
                    emit_compute(items[i])
            if debug:
                p.wait(nc.scalar, "vc")
                p.dma(nc.scalar, d_x3[:], xnewT[:], "sd")
            barrier(nc, p)

        p.vdrain = True
        with ExitStack() as ph:
            Eh = ph.enter_context
            yst = [Eh(nc.sbuf_tensor(f"b5y{i}", [128, DM], F32)) for i in range(2)]
            ps_t = Eh(nc.psum_tensor("b5pt", [128, 128], F32))
            y_sd = [0, 0]
            for tt in range(8):
                i = tt % 2
                if tt >= 2:
                    p.wait(nc.vector, f"so{i}", y_sd[i])
                for dc in range(NDC):
                    t1 = p.inc(nc.tensor.transpose(
                        ps_t[:],
                        xnewT[:, dc * TPC + tt * 128: dc * TPC + (tt + 1) * 128],
                        ident_s[:]), "tm")
                    p.wait(nc.vector, "tm", t1)
                    p.v(lambda: nc.vector.tensor_copy(
                        out=yst[i][:, dc * 128:(dc + 1) * 128], in_=ps_t[:]))
                    p.wait(nc.tensor, "vc", p.c["vc"])
                p.wait(nc.scalar, "vc")
                y_sd[i] = p.dma(nc.scalar, y[tt * 128:(tt + 1) * 128, :], yst[i][:],
                                f"so{i}")
            p.wait(nc.sync, "so0", p.c["so0"])
            p.wait(nc.sync, "so1", p.c["so1"])
    return nc



def kernel(L0_attn_x, L0_attn_Wq, L0_attn_Wk, L0_attn_Wv, L0_attn_Wo,
           L0_ffn_Wr, L0_ffn_Wg, L0_ffn_Wu, L0_ffn_Wd,
           L0_ffn_Wsg, L0_ffn_Wsu, L0_ffn_Wsd):
    x = np.ascontiguousarray(np.asarray(L0_attn_x, dtype=np.float32)[0])
    xT = np.ascontiguousarray(x.T)
    Wq = np.asarray(L0_attn_Wq, dtype=np.float32)
    Wk = np.asarray(L0_attn_Wk, dtype=np.float32)
    Wv = np.asarray(L0_attn_Wv, dtype=np.float32)
    Wo = np.asarray(L0_attn_Wo, dtype=np.float32)

    mask01 = np.tril(np.ones((128, 128), dtype=np.float32), -1)
    identf = np.eye(128, dtype=np.float32)
    wgT = np.ascontiguousarray(np.asarray(L0_ffn_Wg, dtype=np.float32).transpose(0, 2, 1))
    wuT = np.ascontiguousarray(np.asarray(L0_ffn_Wu, dtype=np.float32).transpose(0, 2, 1))
    wdT = np.ascontiguousarray(np.asarray(L0_ffn_Wd, dtype=np.float32).transpose(0, 2, 1))
    wsgT = np.ascontiguousarray(np.asarray(L0_ffn_Wsg, dtype=np.float32).T)
    wsuT = np.ascontiguousarray(np.asarray(L0_ffn_Wsu, dtype=np.float32).T)
    wsdT = np.ascontiguousarray(np.asarray(L0_ffn_Wsd, dtype=np.float32).T)
    wrT = np.ascontiguousarray(np.asarray(L0_ffn_Wr, dtype=np.float32).T)

    ncF = build_full()
    in_maps = []
    for c in range(NCORE):
        csl = slice(c * HPC * HD, (c + 1) * HPC * HD)
        in_maps.append({
            "xT": xT,
            "wq": np.ascontiguousarray(Wq[:, csl]),
            "wk": np.ascontiguousarray(Wk[:, csl]),
            "wv": np.ascontiguousarray(Wv[:, csl]),
            "wo": np.ascontiguousarray(Wo[csl, :]),
            "mask01": mask01,
            "xTm": np.ascontiguousarray(xT[:, c * TPC:(c + 1) * TPC]),
            "wgT": wgT, "wuT": wuT, "wdT": wdT,
            "wsgT": wsgT, "wsuT": wsuT, "wsdT": wsdT, "wrT": wrT,
            "identf": identf,
        })
    res = run_bass_kernel_spmd(ncF, in_maps, core_ids=list(range(NCORE)))
    out = np.concatenate([res.results[c]["y"] for c in range(NCORE)], axis=0)
    return np.ascontiguousarray(out.reshape(1, S, DM), dtype=np.float32)
